# version 11
# baseline (speedup 1.0000x reference)
import math
import numpy as np
import ml_dtypes
from contextlib import ExitStack

import concourse.bass as bass
import concourse.mybir as mybir
from concourse.alu_op_type import AluOpType
from concourse.bass_utils import run_bass_kernel_spmd

F32 = mybir.dt.float32
BF16 = mybir.dt.bfloat16
F16 = mybir.dt.float16
AF = mybir.ActivationFunctionType
AX = mybir.AxisListType

B, S, DM, C = 2, 4096, 4096, 1024
NH, DH, I = 32, 128, 14336
KC = DM // 128
NB = I // 128
NT = S // 128
OT = C // 128
QTN = C // 128
KBN = S // 512
G2 = 8
EPS = 1e-5
QSCALE = float(np.float32(DH ** -0.5))
SENT = -30000.0


def _alibi_slopes(n):
    cp2 = 2 ** math.floor(math.log2(n))
    base = 2.0 ** (-(2.0 ** (-(math.log2(cp2) - 3))))
    slopes = base ** np.arange(1, cp2 + 1, dtype=np.float64)
    if cp2 != n:
        extra_base = 2.0 ** (-(2.0 ** (-(math.log2(2 * cp2) - 3))))
        rem = min(cp2, n - cp2)
        extra = extra_base ** np.arange(1, 1 + 2 * rem, 2, dtype=np.float64)
        slopes = np.concatenate([slopes, extra])
    return slopes.astype(np.float32)


SLOPES = [float(s) for s in _alibi_slopes(NH)]


class Cnt:

    def __init__(self, handle, step):
        self.h = handle
        self.step = step
        self.n = 0

    def bump(self):
        self.n += self.step
        return self.n


class Emit:

    def __init__(self):
        self.streams = {k: [] for k in ("tensor", "vector", "scalar", "gpsimd", "sync")}
        self.waited = {}

    def op(self, eng_name, fn, waits=(), inc=None):
        real_waits = []
        key0 = eng_name
        for cnt, val in waits:
            if val <= 0:
                continue
            k = (key0, id(cnt))
            if self.waited.get(k, -1) >= val:
                continue
            self.waited[k] = val
            real_waits.append((cnt.h, val))
        incv = None
        if inc is not None:
            incv = (inc.h, inc.step)
            inc.n += inc.step

        def run(eng, fn=fn, real_waits=real_waits, incv=incv):
            for h, v in real_waits:
                eng.wait_ge(h, v)
            inst = fn(eng)
            if incv is not None:
                assert inst is not None
                inst.then_inc(incv[0], incv[1])

        self.streams[eng_name].append(run)


def build_program():
    nc = bass.Bass("TRN2", target_bir_lowering=False)

    xb = nc.dram_tensor("xb", [S, DM], F32, kind="ExternalInput").ap()
    xchunk = nc.dram_tensor("xchunk", [C, DM], F32, kind="ExternalInput").ap()
    wq_l = nc.dram_tensor("wq_l", [NH, 128, KC * 128], F32, kind="ExternalInput").ap()
    wkv = nc.dram_tensor("wkv", [DM, 256], F32, kind="ExternalInput").ap()
    wo_l = nc.dram_tensor("wo_l", [8, 128, KC * 512], F32, kind="ExternalInput").ap()
    w1l = nc.dram_tensor("w1l", [NB, 128, KC * 128], F32, kind="ExternalInput").ap()
    w3l = nc.dram_tensor("w3l", [NB, 128, KC * 128], F32, kind="ExternalInput").ap()
    w2t = nc.dram_tensor("w2t", [I, DM], F32, kind="ExternalInput").ap()
    sc1 = nc.dram_tensor("sc1", [128, NB * KC], F32, kind="ExternalInput").ap()
    sc3 = nc.dram_tensor("sc3", [128, NB * KC], F32, kind="ExternalInput").ap()
    sc2 = nc.dram_tensor("sc2", [128, NB * KC], F32, kind="ExternalInput").ap()
    dlt = nc.dram_tensor("dlt", [128, 4096], F16, kind="ExternalInput").ap()
    identb = nc.dram_tensor("identb", [128, 128], BF16, kind="ExternalInput").ap()
    identh = nc.dram_tensor("identh", [128, 128], F16, kind="ExternalInput").ap()
    qscl = nc.dram_tensor("qscl", [128, 1], F32, kind="ExternalInput").ap()

    att_h = nc.dram_tensor("att_h", [DM, C], BF16, kind="Internal").ap()
    s_h = nc.dram_tensor("s_h", [C, DM], BF16, kind="Internal").ap()
    y_h = nc.dram_tensor("y_h", [C, DM], F32, kind="Internal").ap()
    out = nc.dram_tensor("out", [C, DM], F32, kind="ExternalOutput").ap()

    wkv_v = wkv.rearrange("(c p) n -> p c n", p=128)
    w2t_v = w2t.rearrange("(c p) n -> p c n", p=128)
    att_hv = att_h.rearrange("(c p) t -> p c t", p=128)

    em = Emit()

    with ExitStack() as top:
        sem = lambda name, step: Cnt(top.enter_context(nc.semaphore(name)), step)
        qsp = sem("qsp", 16)
        qgp = sem("qgp", 16)
        spe = sem("spe", 1)
        sve = sem("sve", 1)
        sse = sem("sse", 1)
        sge = sem("sge", 1)

        SB = lambda st, name, shape, dt: st.enter_context(nc.sbuf_tensor(name, shape, dt))
        PS = lambda st, name, shape, dt: st.enter_context(nc.psum_tensor(name, shape, dt))

        idb_sb = SB(top, "idb_sb", [128, 128], BF16)
        qscl_sb = SB(top, "qscl_sb", [128, 1], F32)
        idh_sb = SB(top, "idh_sb", [128, 128], F16)
        xnst = ExitStack()
        xnT = SB(xnst, "xnT", [128, KC * 1024], BF16)
        ph01 = ExitStack()
        K_sb = SB(ph01, "K_sb", [128, S], BF16)
        V_sb = SB(ph01, "V_sb", [128, KC * 128], BF16)
        dlt_sb = SB(ph01, "dlt_sb", [128, 4096], F16)

        ph0 = ExitStack()
        xbuf = [SB(ph0, f"xbuf{m}", [128, DM], F32) for m in range(2)]
        sqbuf = SB(ph0, "sqbuf", [128, DM], F32)
        xn_sb = [SB(ph0, f"xn{m}", [128, DM], BF16) for m in range(2)]
        xnt_tmp = [SB(ph0, f"xnt_tmp{m}", [128, KC * 128], BF16) for m in range(2)]
        wkv_sb = SB(ph0, "wkv_sb", [128, KC * 256], BF16)
        stats = {}
        for nm in ("sumx", "sumsq", "negmean", "var", "vareps", "std", "rstd",
                   "negmr"):
            stats[nm] = [SB(ph0, f"{nm}{m}", [128, 1], F32) for m in range(2)]
        tr_ps = [PS(ph0, f"tr_ps{m}", [128, 1024], BF16) for m in range(2)]
        k_ps = [PS(ph0, f"k_ps{m}", [128, 512], F32) for m in range(2)]
        v_ps = [PS(ph0, f"v_ps{m}", [128, 512], F32) for m in range(2)]

        em.op("gpsimd", lambda g: g.dma_start(
            wkv_sb[:].rearrange("p (c n) -> p c n", c=KC), wkv_v),
            inc=qgp)
        q_wkv = qgp.n
        em.op("gpsimd", lambda g: g.dma_start(dlt_sb[:], dlt), inc=qgp)
        em.op("gpsimd", lambda g: g.dma_start(idb_sb[:], identb), inc=qgp)
        em.op("gpsimd", lambda g: g.dma_start(idh_sb[:], identh), inc=qgp)
        em.op("gpsimd", lambda g: g.dma_start(qscl_sb[:], qscl), inc=qgp)
        q_const = qgp.n

        mark = {}

        def phase0_tile(i, src_rows, own):
            m = i % 2
            waits = []
            if mark.get(("xapply", i - 2)) is not None:
                waits.append((sse, mark[("xapply", i - 2)]))
            em.op("sync", lambda sy, sr=src_rows: sy.dma_start(xbuf[m][:], sr),
                  waits=waits, inc=qsp)
            q_x = qsp.n
            waits = [(qsp, q_x)]
            if ("stats", i - 2) in mark:
                waits.append((sve, mark[("stats", i - 2)]))
            em.op("scalar", lambda s: s.activation(
                sqbuf[:], xbuf[m][:], AF.Square, accum_out=stats["sumsq"][m][:]),
                waits=waits, inc=sse)
            a0 = sse.n
            em.op("vector", lambda v: v.tensor_reduce(
                stats["sumx"][m][:], xbuf[m][:], AX.X, AluOpType.add),
                waits=waits, inc=sve)
            em.op("vector", lambda v: v.tensor_scalar(
                stats["negmean"][m][:], stats["sumx"][m][:], -1.0 / DM, None,
                AluOpType.mult), inc=sve)
            g2 = sve.n
            em.op("vector", lambda v: v.scalar_tensor_tensor(
                stats["var"][m][:], stats["sumx"][m][:], stats["negmean"][m][:],
                stats["sumsq"][m][:], AluOpType.mult, AluOpType.add),
                waits=[(sve, g2), (sse, a0)], inc=sve)
            g3 = sve.n
            em.op("vector", lambda v: v.tensor_scalar(
                stats["vareps"][m][:], stats["var"][m][:], 1.0 / DM, EPS,
                AluOpType.mult, AluOpType.add), waits=[(sve, g3)], inc=sve)
            g3b = sve.n
            em.op("scalar", lambda s: s.activation(
                stats["std"][m][:], stats["vareps"][m][:], AF.Sqrt),
                waits=[(sve, g3b)], inc=sse)
            a1 = sse.n
            em.op("vector", lambda v: v.reciprocal(
                stats["rstd"][m][:], stats["std"][m][:]), waits=[(sse, a1)], inc=sve)
            g4 = sve.n
            em.op("vector", lambda v: v.tensor_scalar(
                stats["negmr"][m][:], stats["sumx"][m][:], stats["rstd"][m][:],
                -1.0 / DM, AluOpType.mult, AluOpType.mult),
                waits=[(sve, g4)], inc=sve)
            g5 = sve.n
            mark[("stats", i)] = g5
            waits = [(sve, g5)]
            if ("tr", i - 2, 3) in mark:
                waits.append((spe, mark[("tr", i - 2, 3)]))
            em.op("scalar", lambda s: s.activation(
                xn_sb[m][:], xbuf[m][:], AF.Identity,
                bias=stats["negmr"][m][:], scale=stats["rstd"][m][:]),
                waits=waits, inc=sse)
            a2 = sse.n
            mark[("xapply", i)] = a2
            for r in range(4):
                mb2 = r % 2
                waits = [(sse, a2), (qgp, q_const)]
                if ("trevac", i, r - 2) in mark:
                    waits.append(mark[("trevac", i, r - 2)])
                elif ("trevac", i - 1, r + 2) in mark:
                    waits.append(mark[("trevac", i - 1, r + 2)])
                for cc in range(8):
                    c = r * 8 + cc
                    em.op("tensor", lambda t, c=c, cc=cc, mb2=mb2, m=m: t.matmul(
                        tr_ps[mb2][:, cc * 128:(cc + 1) * 128],
                        xn_sb[m][:, c * 128:(c + 1) * 128], idb_sb[:],
                        start=(cc == 0), stop=(cc == 7), is_transpose=True),
                        waits=(waits if cc == 0 else ()),
                        inc=(spe if cc == 7 else None))
                mark[("tr", i, r)] = spe.n
                if own:
                    dst = xnT[:].rearrange("p (c t) -> p c t", t=1024)[
                        :, r * 8:(r + 1) * 8, (i % 8) * 128:((i % 8) + 1) * 128]
                else:
                    dst = xnt_tmp[m][:, r * 1024:(r + 1) * 1024].rearrange(
                        "p (c t) -> p c t", t=128)
                if r < 2:
                    em.op("vector", lambda v, dst=dst, mb2=mb2: v.tensor_copy(
                        dst, tr_ps[mb2][:].rearrange("p (c t) -> p c t", t=128)),
                        waits=[(spe, mark[("tr", i, r)])], inc=sve)
                    mark[("trevac", i, r)] = (sve, sve.n)
                else:
                    em.op("scalar", lambda s, dst=dst, mb2=mb2: s.copy(
                        dst, tr_ps[mb2][:].rearrange("p (c t) -> p c t", t=128)),
                        waits=[(spe, mark[("tr", i, r)])], inc=sse)
                    mark[("trevac", i, r)] = (sse, sse.n)
            if own:
                return
            waits = [mark[("trevac", i, 1)], mark[("trevac", i, 3)], (qgp, q_wkv)]
            if ("kve", i - 2) in mark:
                waits.append((sse, mark[("kve", i - 2)]))
            for c in range(KC):
                src_c = xnt_tmp[m][:, c * 128:(c + 1) * 128]
                em.op("tensor", lambda t, c=c, src_c=src_c: t.matmul(
                    k_ps[m][:, 0:128], wkv_sb[:, c * 256:c * 256 + 128], src_c,
                    start=(c == 0), stop=(c == KC - 1)),
                    waits=(waits if c == 0 else ()),
                    inc=(spe if c == KC - 1 else None))
            for c in range(KC):
                src_c = xnt_tmp[m][:, c * 128:(c + 1) * 128]
                em.op("tensor", lambda t, c=c, src_c=src_c: t.matmul(
                    v_ps[m][:, 0:128], src_c, wkv_sb[:, c * 256 + 128:c * 256 + 256],
                    start=(c == 0), stop=(c == KC - 1)),
                    inc=(spe if c == KC - 1 else None))
            mark[("kv", i)] = spe.n
            em.op("scalar", lambda s: s.copy(
                K_sb[:, i * 128:(i + 1) * 128], k_ps[m][:, 0:128]),
                waits=[(spe, mark[("kv", i)])])
            em.op("scalar", lambda s: s.copy(
                V_sb[:, i * 128:(i + 1) * 128], v_ps[m][:, 0:128]), inc=sse)
            mark[("kve", i)] = sse.n

        for i in range(NT):
            phase0_tile(i, xb[i * 128:(i + 1) * 128, :], own=False)
        for i in range(NT, NT + OT):
            phase0_tile(i, xchunk[(i - NT) * 128:(i - NT + 1) * 128, :], own=True)

        ph0_done_se = sse.n
        ph0_done_ve = sve.n
        ph0_done_pe = spe.n
        ph0.close()

        ph1 = ExitStack()
        wq_sb = [SB(ph1, f"wq{m}", [128, KC * 128], BF16) for m in range(2)]
        qT_sb = [SB(ph1, f"qT{m}", [128, C], BF16) for m in range(2)]
        t_sb = [SB(ph1, f"t{m}", [128, S], F16) for m in range(2)]
        p_sb = [SB(ph1, f"pp{m}", [128, S], F16) for m in range(2)]
        ptA = SB(ph1, "ptA", [128, 16 * 512], BF16)
        ptB = SB(ph1, "ptB", [128, 32 * 512], BF16)
        negmx = [SB(ph1, f"negmx{m}", [128, 1], F32) for m in range(2)]
        den = [SB(ph1, f"den{m}", [128, 1], F32) for m in range(2)]
        rec = [SB(ph1, f"rec{m}", [128, 1], F32) for m in range(2)]
        d_sb = [SB(ph1, f"dd{m}", [128, 128], F16) for m in range(2)]
        att_st = [SB(ph1, f"attst{m}", [128, C], BF16) for m in range(2)]
        q_ps = [PS(ph1, f"q_ps{m}", [128, 512], F32) for m in range(2)]
        sc_ps = [PS(ph1, f"sc_ps{m}", [128, 512], F32) for m in range(2)]
        pt_ps = [PS(ph1, f"pt_ps{m}", [128, 512], F32) for m in range(2)]
        av_ps = [PS(ph1, f"av_ps{m}", [128, 512], F32) for m in range(2)]

        em.op("vector", lambda v: v.memset(ptA[:], 0.0),
              waits=[(sse, ph0_done_se), (sve, ph0_done_ve), (spe, ph0_done_pe)])
        em.op("vector", lambda v: v.memset(ptB[:], 0.0), inc=sve)
        pt_zeroed = sve.n

        def emit_head(h):
            m = h % 2
            if h < 2:
                waits = [(sse, ph0_done_se), (sve, ph0_done_ve),
                         (spe, ph0_done_pe)]
            else:
                waits = [(spe, mark[("qp", h - 2)])]
            em.op("gpsimd", lambda g: g.dma_start(
                wq_sb[m][:], wq_l[h]), waits=waits, inc=qgp)
            mark[("wq", h)] = qgp.n

        def emit_qp_chunk(h, c0, c1):
            m = h % 2
            waits = []
            if c0 == 0:
                waits = [(qgp, mark[("wq", h)]), (sse, ph0_done_se),
                         (sve, ph0_done_ve)]
                if ("qe", h - 1) in mark:
                    waits.append(mark[("qe", h - 1)])
            first = True
            for c in range(c0, c1):
                for half in range(2):
                    em.op("tensor", lambda t, half=half, c=c, m=m: t.matmul(
                        q_ps[half][:],
                        wq_sb[m][:, c * 128:(c + 1) * 128],
                        xnT[:, c * 1024 + half * 512: c * 1024 + half * 512 + 512],
                        start=(c == 0), stop=(c == KC - 1)),
                        waits=(waits if first else ()),
                        inc=(spe if (c == c1 - 1 and half == 1) else None))
                    first = False
            if c1 == KC:
                mark[("qp", h)] = spe.n

        def emit_qe(h):
            m = h % 2
            for half in range(2):
                em.op("scalar", lambda s, half=half, m=m: s.activation(
                    qT_sb[m][:, half * 512:(half + 1) * 512], q_ps[half][:],
                    AF.Identity, scale=qscl_sb[:]),
                    waits=([(spe, mark[("qp", h)]), (qgp, q_const)]
                           if half == 0 else ()),
                    inc=(sse if half == 1 else None))
            mark[("qe", h)] = (sse, sse.n)

        def emit_head_body(h):
            m = h % 2
            slope = SLOPES[h]
            emit_qe(h)
            def emit_sc(qt):
                mq = qt % 2
                for kb in range(qt + 1):
                    bank = len(seq["stt"]) % 2
                    waits = [mark[("qe", h)], (sve, pt_zeroed)]
                    if len(seq["stt"]) >= 2:
                        waits.append((sve, seq["stt"][-2]))
                    em.op("tensor", lambda t, qt=qt, kb=kb, bank=bank: t.matmul(
                        sc_ps[bank][:],
                        qT_sb[m][:, qt * 128:(qt + 1) * 128],
                        K_sb[:, kb * 512:(kb + 1) * 512],
                        start=True, stop=True),
                        waits=waits, inc=spe)
                    mark[("sc", h, qt, kb)] = spe.n
                    off = (kb - qt) * 512 + 3584
                    waits = [(spe, mark[("sc", h, qt, kb)])]
                    if kb == 0:
                        pe_exp = mark.get(("exp", h, qt - 2)) or mark.get(
                            ("exp", h - 1, qt + QTN - 2))
                        if pe_exp:
                            waits.append((sse, pe_exp))
                    em.op("vector", lambda v, qt=qt, kb=kb, off=off, mq=mq,
                          bank=bank: v.scalar_tensor_tensor(
                              t_sb[mq][:, kb * 512:(kb + 1) * 512],
                              dlt_sb[:, off:off + 512], slope, sc_ps[bank][:],
                              AluOpType.mult, AluOpType.add),
                          waits=waits, inc=sve)
                    seq["stt"].append(sve.n)
                em.op("vector", lambda v, mq=mq, qt=qt: v.tensor_reduce(
                    negmx[mq][:], t_sb[mq][:, 0:(qt + 1) * 512], AX.X,
                    AluOpType.max, negate=True),
                    waits=[(sve, seq["stt"][-1])], inc=sve)
                mark[("mx", h, qt)] = sve.n
                waits = [(sve, mark[("mx", h, qt)])]
                pe_pt = mark.get(("ptdone", h, qt - 2)) or mark.get(
                    ("ptdone", h - 1, qt + QTN - 2))
                if pe_pt:
                    waits.append((spe, pe_pt))
                em.op("scalar", lambda s, mq=mq, qt=qt: s.activation(
                    p_sb[mq][:, 0:(qt + 1) * 512], t_sb[mq][:, 0:(qt + 1) * 512],
                    AF.Exp, bias=negmx[mq][:], accum_out=den[mq][:]),
                    waits=waits, inc=sse)
                mark[("exp", h, qt)] = sse.n

            def emit_rd(qt):
                mq = qt % 2
                em.op("vector", lambda v, mq=mq: v.reciprocal(
                    rec[mq][:], den[mq][:]),
                    waits=[(sse, mark[("exp", h, qt)])], inc=sve)
                rr = sve.n
                waits = [(sve, rr)]
                pe_pt = mark.get(("ptdone", h, qt - 2)) or mark.get(
                    ("ptdone", h - 1, qt + QTN - 2))
                if pe_pt:
                    waits.append((spe, pe_pt))
                em.op("vector", lambda v, mq=mq: v.tensor_scalar(
                    d_sb[mq][:], idh_sb[:], rec[mq][:], None, AluOpType.mult),
                    waits=waits, inc=sve)
                mark[("dd", h, qt)] = sve.n

            def emit_pt(qt):
                mq = qt % 2
                qg = qt // 4
                dstbuf = ptA if qg == 0 else ptB
                for g in range(qt + 1):
                    bank = len(seq["pte"]) % 2
                    waits = [(sse, mark[("exp", h, qt)]), (sve, mark[("dd", h, qt)])]
                    if len(seq["pte"]) >= 2:
                        waits.append((sse, seq["pte"][-2]))
                    for kk in range(4):
                        ks = g * 4 + kk
                        em.op("tensor", lambda t, ks=ks, kk=kk, mq=mq,
                              bank=bank: t.matmul(
                            pt_ps[bank][:, kk * 128:(kk + 1) * 128],
                            p_sb[mq][:, ks * 128:(ks + 1) * 128], d_sb[mq][:],
                            start=(kk == 0), stop=(kk == 3)),
                            waits=(waits if kk == 0 else ()),
                            inc=(spe if kk == 3 else None))
                    mark[("ptg", h, qt, g)] = spe.n
                    dst = dstbuf[:].rearrange("p (c q) -> p c q", q=512)[
                        :, g * 4:(g + 1) * 4, (qt % 4) * 128:((qt % 4) + 1) * 128]
                    waits = [(spe, mark[("ptg", h, qt, g)])]
                    if qt % 4 == 0 and g == 0:
                        prev_av = mark.get(("av", h, qt // 4 - 1)) or mark.get(
                            ("av", h - 1, qt // 4))
                        if prev_av:
                            waits.append((spe, prev_av))
                    em.op("scalar", lambda s, dst=dst, bank=bank: s.copy(
                        dst, pt_ps[bank][:].rearrange("p (c q) -> p c q", q=128)),
                        waits=waits, inc=sse)
                    seq["pte"].append(sse.n)
                mark[("ptdone", h, qt)] = spe.n
                mark[("ptedone", h, qt)] = sse.n

            def emit_av(qg):
                srcbuf = ptA if qg == 0 else ptB
                ncks = 16 * qg + 16
                waits = [(sse, mark[("ptedone", h, qg * 4 + 3)])]
                prev = mark.get(("ave", h, qg - 2)) or mark.get(("ave", h - 1, qg))
                if prev:
                    waits.append((sse, prev))
                for ck in range(ncks):
                    em.op("tensor", lambda t, ck=ck, qg=qg, srcbuf=srcbuf: t.matmul(
                        av_ps[qg % 2][:],
                        V_sb[:, ck * 128:(ck + 1) * 128],
                        srcbuf[:, ck * 512:(ck + 1) * 512],
                        start=(ck == 0), stop=(ck == ncks - 1)),
                        waits=(waits if ck == 0 else ()),
                        inc=(spe if ck == ncks - 1 else None))
                mark[("av", h, qg)] = spe.n
                waits = [(spe, mark[("av", h, qg)])]
                if ("att_store", h - 2) in mark:
                    waits.append((qsp, mark[("att_store", h - 2)]))
                em.op("scalar", lambda s, qg=qg: s.copy(
                    att_st[m][:, qg * 512:(qg + 1) * 512], av_ps[qg % 2][:]),
                    waits=waits, inc=sse)
                mark[("ave", h, qg)] = sse.n

            emit_sc(0)
            for qt in range(1, QTN):
                emit_sc(qt)
                emit_rd(qt - 1)
                emit_pt(qt - 1)
                if qt - 1 == 3:
                    emit_av(0)
                if qt >= 4 and h + 1 < NH:
                    emit_qp_chunk(h + 1, (qt - 4) * 8, (qt - 3) * 8)
            emit_rd(QTN - 1)
            emit_pt(QTN - 1)
            emit_av(1)
            em.op("sync", lambda sy: sy.dma_start(
                att_h[h * 128:(h + 1) * 128, :], att_st[m][:]),
                waits=[(sse, mark[("ave", h, 1)])], inc=qsp)
            mark[("att_store", h)] = qsp.n

        seq = {"stt": [], "pte": []}
        emit_head(0)
        emit_head(1)
        emit_qp_chunk(0, 0, KC)
        for h in range(NH):
            if h + 2 < NH:
                emit_head(h + 2)
            emit_head_body(h)

        ph1_stores = qsp.n
        ph1.close()
        ph01.close()

        ph2 = ExitStack()
        att_all = SB(ph2, "att_all", [128, KC * 1024], BF16)
        wo_sb = [SB(ph2, f"wo{m}", [128, KC * 512], BF16) for m in range(2)]
        s_st = [SB(ph2, f"s_st{m}", [128, 512], BF16) for m in range(2)]
        s_ps = [PS(ph2, f"s_ps{m}", [128, 512], F32) for m in range(2)]

        em.op("sync", lambda sy: sy.dma_start(
            att_all[:].rearrange("p (c t) -> p c t", t=1024), att_hv),
            waits=[(qsp, ph1_stores)], inc=qsp)
        q_attall = qsp.n

        def emit_wo_load(dt):
            mw = dt % 2
            if dt < 2:
                waits = [(qsp, ph1_stores)]
            else:
                waits = [(spe, mark[("wopd", dt - 2)])]
            em.op("gpsimd", lambda g, dt=dt, mw=mw: g.dma_start(
                wo_sb[mw][:], wo_l[dt]), waits=waits, inc=qgp)
            mark[("wol", dt)] = qgp.n

        emit_wo_load(0)
        emit_wo_load(1)
        for dt in range(8):
            mw = dt % 2
            for qt in range(QTN):
                waits = [(qgp, mark[("wol", dt)]), (qsp, q_attall)]
                prev = mark.get(("se2", dt, qt - 2)) or mark.get(("se2", dt - 1, qt + 6))
                if prev:
                    waits.append((sse, prev))
                for hc in range(KC):
                    em.op("tensor", lambda t, dt=dt, qt=qt, hc=hc, mw=mw: t.matmul(
                        s_ps[qt % 2][:],
                        att_all[:, hc * 1024 + qt * 128: hc * 1024 + qt * 128 + 128],
                        wo_sb[mw][:, hc * 512:(hc + 1) * 512],
                        start=(hc == 0), stop=(hc == KC - 1)),
                        waits=(waits if hc == 0 else ()),
                        inc=(spe if hc == KC - 1 else None))
                mark[("wopq", dt, qt)] = spe.n
                waits = [(spe, mark[("wopq", dt, qt)])]
                prev = mark.get(("s_store", dt, qt - 2)) or mark.get(
                    ("s_store", dt - 1, qt + 6))
                if prev:
                    waits.append((qsp, prev))
                em.op("scalar", lambda s, qt=qt: s.copy(
                    s_st[qt % 2][:], s_ps[qt % 2][:]), waits=waits, inc=sse)
                mark[("se2", dt, qt)] = sse.n
                em.op("sync", lambda sy, dt=dt, qt=qt: sy.dma_start(
                    s_h[qt * 128:(qt + 1) * 128, dt * 512:(dt + 1) * 512],
                    s_st[qt % 2][:]),
                    waits=[(sse, mark[("se2", dt, qt)])], inc=qsp)
                mark[("s_store", dt, qt)] = qsp.n
            mark[("wopd", dt)] = spe.n
            if dt + 2 < 8:
                emit_wo_load(dt + 2)

        s_stores = qsp.n
        ph2.close()

        ph3 = ExitStack()
        scs_sb = [SB(ph3, f"scs{m}", [128, 3 * KC], F32) for m in range(2)]
        w1r = [SB(ph3, f"w1r{m}", [128, KC * 128], BF16) for m in range(2)]
        w3r = [SB(ph3, f"w3r{m}", [128, KC * 128], BF16) for m in range(2)]
        w2g = SB(ph3, "w2g", [128, G2 * DM], BF16)
        hT = SB(ph3, "hT", [128, G2 * 1024], BF16)
        gs_tmp = SB(ph3, "gs_tmp", [128, 1024], F32)
        y_st = [SB(ph3, f"y_st{m}", [128, 2048], F32) for m in range(2)]
        g_ps = PS(ph3, "g_ps", [128, 1024], F32)
        u_ps = PS(ph3, "u_ps", [128, 1024], F32)
        y_ps = [PS(ph3, f"y_ps{m}", [128, 512], F32) for m in range(2)]
        y_ps2 = [PS(ph3, f"y_ps2{m}", [128, 512], F32) for m in range(2)]

        def dq_fn(v, buf, scoff, msc, nb):
            t3 = buf.rearrange("p (c n) -> p c n", n=128)
            scb = scs_sb[msc][:, scoff * KC:(scoff + 1) * KC].unsqueeze(
                2).broadcast_to([128, KC, 128])
            return v.tensor_tensor(t3, t3, scb, AluOpType.mult)

        def emit_w13(nb):
            m = nb % 2
            waits = []
            if ("f1h", nb - 2, 1) in mark:
                waits.append((spe, mark[("f1h", nb - 2, 1)]))
            else:
                waits.append((qsp, s_stores))
            swaits = list(waits)
            if ("dq2", nb - 2) in mark:
                swaits.append((sve, mark[("dq2", nb - 2)]))
            em.op("sync", lambda sy, nb=nb, m=m: sy.dma_start(
                scs_sb[m][:, 0:KC], sc1[:, nb * KC:(nb + 1) * KC]),
                waits=swaits, inc=qsp)
            em.op("sync", lambda sy, nb=nb, m=m: sy.dma_start(
                scs_sb[m][:, KC:2 * KC], sc3[:, nb * KC:(nb + 1) * KC]), inc=qsp)
            em.op("sync", lambda sy, nb=nb, m=m: sy.dma_start(
                scs_sb[m][:, 2 * KC:3 * KC], sc2[:, nb * KC:(nb + 1) * KC]),
                inc=qsp)
            mark[("scl", nb)] = qsp.n
            em.op("gpsimd", lambda g, nb=nb, m=m: g.dma_start(
                w1r[m][:], w1l[nb]), waits=waits, inc=qgp)
            em.op("gpsimd", lambda g, nb=nb, m=m: g.dma_start(
                w3r[m][:], w3l[nb]), inc=qgp)
            q_w = qgp.n
            em.op("vector", lambda v, m=m, nb=nb: dq_fn(v, w1r[m][:], 0, m, nb),
                  waits=[(qgp, q_w), (qsp, mark[("scl", nb)])])
            em.op("vector", lambda v, m=m, nb=nb: dq_fn(v, w3r[m][:], 1, m, nb),
                  inc=sve)
            mark[("dq13", nb)] = sve.n

        def emit_w2(nb):
            gidx = nb // G2
            slot = nb % G2
            m = nb % 2
            waits = []
            if ("f2", gidx - 1) in mark:
                waits.append((spe, mark[("f2", gidx - 1)]))
            elif gidx == 0:
                waits.append((qsp, s_stores))
            em.op("gpsimd", lambda g, nb=nb, slot=slot: g.dma_start(
                w2g[:, slot * DM:(slot + 1) * DM], w2t_v[:, nb, :]),
                waits=waits, inc=qgp)
            q_w = qgp.n
            em.op("vector", lambda v, slot=slot, nb=nb, m=m: dq_fn(
                v, w2g[:, slot * DM:(slot + 1) * DM], 2, m, nb),
                waits=[(qgp, q_w), (qsp, mark[("scl", nb)])], inc=sve)
            mark[("dq2", nb)] = sve.n

        emit_w13(0)
        emit_w2(0)
        for nb in range(NB):
            m = nb % 2
            gidx = nb // G2
            slot = nb % G2
            if nb + 1 < NB:
                emit_w13(nb + 1)
            for half in range(2):
                waits = [(sve, mark[("dq13", nb)]), (sse, ph0_done_se),
                         (sve, ph0_done_ve)]
                if ("hm", nb - 1, half) in mark:
                    waits.append((sve, mark[("hm", nb - 1, half)]))
                if slot == 0 and half == 0 and ("ye", gidx - 1, 7, 7) in mark:
                    waits.append((sse, mark[("ye", gidx - 1, 7, 7)]))
                first = True
                for c in range(KC):
                    em.op("tensor", lambda t, c=c, half=half, m=m: t.matmul(
                        g_ps[:, half * 512:(half + 1) * 512],
                        w1r[m][:, c * 128:(c + 1) * 128],
                        xnT[:, c * 1024 + half * 512: c * 1024 + half * 512 + 512],
                        start=(c == 0), stop=(c == KC - 1)),
                        waits=(waits if first else ())); first = False
                    em.op("tensor", lambda t, c=c, half=half, m=m: t.matmul(
                        u_ps[:, half * 512:(half + 1) * 512],
                        w3r[m][:, c * 128:(c + 1) * 128],
                        xnT[:, c * 1024 + half * 512: c * 1024 + half * 512 + 512],
                        start=(c == 0), stop=(c == KC - 1)),
                        inc=(spe if c == KC - 1 else None))
                mark[("f1h", nb, half)] = spe.n
                em.op("scalar", lambda s, half=half: s.activation(
                    gs_tmp[:, half * 512:(half + 1) * 512],
                    g_ps[:, half * 512:(half + 1) * 512], AF.Silu),
                    waits=[(spe, mark[("f1h", nb, half)])], inc=sse)
                mark[("si", nb, half)] = sse.n
                em.op("vector", lambda v, slot=slot, half=half: v.tensor_tensor(
                    hT[:, slot * 1024 + half * 512: slot * 1024 + half * 512 + 512],
                    gs_tmp[:, half * 512:(half + 1) * 512],
                    u_ps[:, half * 512:(half + 1) * 512],
                    AluOpType.mult), waits=[(sse, mark[("si", nb, half)])],
                    inc=sve)
                mark[("hm", nb, half)] = sve.n
            if slot == G2 - 1:
                g = gidx
                ybank = [g_ps[:, 0:512], g_ps[:, 512:1024],
                         u_ps[:, 0:512], u_ps[:, 512:1024],
                         y_ps[0][:], y_ps[1][:], y_ps2[0][:], y_ps2[1][:]]
                for tb in range(8):
                    for nn in range(G2):
                        for dt in range(8):
                            waits = []
                            if nn == 0:
                                waits = [(sve, mark[("hm", nb, 1)]),
                                         (sve, mark[("dq2", nb)])]
                                ye_prev = mark.get(("ye", g, tb - 1, dt)) or \
                                    mark.get(("ye", g - 1, 7, dt))
                                if ye_prev:
                                    waits.append((sse, ye_prev))
                            em.op("tensor", lambda t, tb=tb, dt=dt, nn=nn: t.matmul(
                                ybank[dt],
                                hT[:, nn * 1024 + tb * 128: nn * 1024 + tb * 128 + 128],
                                w2g[:, nn * DM + dt * 512: nn * DM + (dt + 1) * 512],
                                start=(nn == 0), stop=(nn == G2 - 1)),
                                waits=waits,
                                inc=(spe if (nn == G2 - 1 and dt == 7) else None))
                    mark[("f2t", g, tb)] = spe.n
                    for dt in range(8):
                        half = dt // 4
                        ybuf = (tb * 2 + half) % 2
                        waits = [(spe, mark[("f2t", g, tb)])]
                        if dt % 4 == 0:
                            prev = mark.get(("y_store", g, tb - 1, half)) or \
                                mark.get(("y_store", g - 1, 7, half))
                            if prev:
                                waits.append((qgp, prev))
                        em.op("scalar", lambda s, dt=dt, ybuf=ybuf: s.copy(
                            y_st[ybuf][:, (dt % 4) * 512:((dt % 4) + 1) * 512],
                            ybank[dt]), waits=waits, inc=sse)
                        mark[("ye", g, tb, dt)] = sse.n
                        if dt % 4 == 3:
                            em.op("gpsimd", lambda gp, tb=tb, g=g, half=half,
                                  ybuf=ybuf: gp.dma_start(
                                      y_h[tb * 128:(tb + 1) * 128,
                                          half * 2048:(half + 1) * 2048],
                                      y_st[ybuf][:],
                                      **({"accum_op": AluOpType.add}
                                         if g > 0 else {})),
                                  waits=[(sse, mark[("ye", g, tb, dt)])], inc=qgp)
                            mark[("y_store", g, tb, half)] = qgp.n
                mark[("f2", g)] = spe.n
            if nb + 1 < NB:
                emit_w2(nb + 1)

        y_stores_gp = qgp.n
        ph3.close()
        xnst.close()

        ph4 = ExitStack()
        y_in = [SB(ph4, f"y_in{m}", [128, DM], F32) for m in range(2)]
        si_in = [SB(ph4, f"si_in{m}", [128, DM], BF16) for m in range(2)]
        x_in = [SB(ph4, f"x_in{m}", [128, DM], F32) for m in range(2)]
        gf_tmp = [SB(ph4, f"gf{m}", [128, DM], F32) for m in range(2)]
        sg_tmp = [SB(ph4, f"sg{m}", [128, DM], F32) for m in range(2)]
        out_t = [SB(ph4, f"out_t{m}", [128, DM], F32) for m in range(2)]

        for tb in range(OT):
            m = tb % 2
            waits = [(qgp, y_stores_gp), (qsp, s_stores)]
            if ("cmb", tb - 2) in mark:
                waits.append((sve, mark[("cmb", tb - 2)]))
            em.op("sync", lambda sy, tb=tb, m=m: sy.dma_start(
                y_in[m][:], y_h[tb * 128:(tb + 1) * 128, :]), waits=waits, inc=qsp)
            em.op("sync", lambda sy, tb=tb, m=m: sy.dma_start(
                si_in[m][:], s_h[tb * 128:(tb + 1) * 128, :]), inc=qsp)
            em.op("sync", lambda sy, tb=tb, m=m: sy.dma_start(
                x_in[m][:], xchunk[tb * 128:(tb + 1) * 128, :]), inc=qsp)
            q_c = qsp.n
            waits = [(qsp, q_c)]
            if ("cmb_o", tb - 2) in mark:
                waits.append((qsp, mark[("cmb_o", tb - 2)]))
            em.op("scalar", lambda s, m=m: s.activation(
                gf_tmp[m][:], y_in[m][:], AF.Silu), waits=waits, inc=sse)
            c1 = sse.n
            em.op("vector", lambda v, m=m: v.tensor_tensor(
                sg_tmp[m][:], si_in[m][:], gf_tmp[m][:], AluOpType.mult),
                waits=[(sse, c1)], inc=sve)
            c2 = sve.n
            em.op("vector", lambda v, m=m: v.tensor_tensor(
                out_t[m][:], sg_tmp[m][:], x_in[m][:], AluOpType.add),
                waits=[(sve, c2)], inc=sve)
            mark[("cmb", tb)] = sve.n
            em.op("sync", lambda sy, tb=tb, m=m: sy.dma_start(
                out[tb * 128:(tb + 1) * 128, :], out_t[m][:]),
                waits=[(sve, mark[("cmb", tb)])], inc=qsp)
            mark[("cmb_o", tb)] = qsp.n
        ph4.close()

        total_qsp = qsp.n
        total_qgp = qgp.n

        def final_wait(eng):
            eng.wait_ge(qsp.h, total_qsp)
            eng.wait_ge(qgp.h, total_qgp)
        em.streams["sync"].append(final_wait)

        with nc.Block() as blk:
            @blk.tensor
            def _(eng):
                for fn in em.streams["tensor"]:
                    fn(eng)

            @blk.vector
            def _(eng):
                for fn in em.streams["vector"]:
                    fn(eng)

            @blk.scalar
            def _(eng):
                for fn in em.streams["scalar"]:
                    fn(eng)

            @blk.gpsimd
            def _(eng):
                for fn in em.streams["gpsimd"]:
                    fn(eng)

            @blk.sync
            def _(eng):
                for fn in em.streams["sync"]:
                    fn(eng)

    return nc


def _host_inputs(L0_attn_hidden_states, L0_attn_Wqkv, L0_attn_Wo,
                 L0_ffn_w1, L0_ffn_w1_scale, L0_ffn_w3, L0_ffn_w3_scale,
                 L0_ffn_w2, L0_ffn_w2_scale):
    x = np.ascontiguousarray(np.asarray(L0_attn_hidden_states, dtype=np.float32))
    wqkv = np.asarray(L0_attn_Wqkv, dtype=np.float32)
    wo = np.asarray(L0_attn_Wo, dtype=np.float32)

    def tile_kn(wt, nblocks):
        K, N = wt.shape
        return np.ascontiguousarray(
            wt.reshape(K // 128, 128, nblocks, N // nblocks)
            .transpose(2, 1, 0, 3).reshape(nblocks, 128, -1))

    wq_l = tile_kn(wqkv[:, :4096], NH)
    wkv = np.ascontiguousarray(wqkv[:, 4096:])
    wo_l = tile_kn(wo, 8)
    w1l = tile_kn(np.asarray(L0_ffn_w1, dtype=np.float32).T, NB)
    w3l = tile_kn(np.asarray(L0_ffn_w3, dtype=np.float32).T, NB)
    w2t = np.ascontiguousarray(np.asarray(L0_ffn_w2, dtype=np.float32).T)
    s1 = np.asarray(L0_ffn_w1_scale, dtype=np.float32)
    s3 = np.asarray(L0_ffn_w3_scale, dtype=np.float32)
    s2 = np.asarray(L0_ffn_w2_scale, dtype=np.float32)
    sc1 = np.ascontiguousarray(np.broadcast_to(s1.reshape(1, -1), (128, NB * KC)))
    sc3 = np.ascontiguousarray(np.broadcast_to(s3.reshape(1, -1), (128, NB * KC)))
    sc2 = np.ascontiguousarray(np.broadcast_to(
        np.ascontiguousarray(s2.T).reshape(1, -1), (128, NB * KC)))
    identb = np.eye(128, dtype=ml_dtypes.bfloat16)
    identh = np.eye(128, dtype=np.float16)

    in_maps = []
    for core in range(8):
        b, j = core // 4, core % 4
        rows = np.concatenate([np.arange(512 * k + 128 * j, 512 * k + 128 * j + 128)
                               for k in range(QTN)])
        cc = np.arange(4096, dtype=np.float64)[None, :]
        pq = np.arange(128, dtype=np.float64)[:, None]
        v = (cc - 3584.0 - 128.0 * j) - pq
        tmpl = np.where(v <= 0, v, SENT).astype(np.float16)
        in_maps.append(dict(
            xb=np.ascontiguousarray(x[b]),
            xchunk=np.ascontiguousarray(x[b][rows]),
            wq_l=wq_l, wkv=wkv, wo_l=wo_l, w1l=w1l, w3l=w3l, w2t=w2t,
            sc1=sc1, sc3=sc3, sc2=sc2,
            dlt=tmpl, identb=identb, identh=identh,
            qscl=np.full((128, 1), QSCALE, dtype=np.float32),
        ))
    return in_maps


def kernel(L0_attn_hidden_states, L0_attn_Wqkv, L0_attn_Wo,
           L0_ffn_w1, L0_ffn_w1_scale, L0_ffn_w3, L0_ffn_w3_scale,
           L0_ffn_w2, L0_ffn_w2_scale):
    in_maps = _host_inputs(L0_attn_hidden_states, L0_attn_Wqkv, L0_attn_Wo,
                           L0_ffn_w1, L0_ffn_w1_scale, L0_ffn_w3, L0_ffn_w3_scale,
                           L0_ffn_w2, L0_ffn_w2_scale)
    nc = build_program()
    res = run_bass_kernel_spmd(nc, in_maps, core_ids=list(range(8)))
    out = np.empty((B, S, DM), dtype=np.float32)
    for core in range(8):
        b, j = core // 4, core % 4
        rows = np.concatenate([np.arange(512 * k + 128 * j, 512 * k + 128 * j + 128)
                               for k in range(QTN)])
        out[b][rows] = np.asarray(res.results[core]["out"], dtype=np.float32)
    return out


# version 16
# speedup vs baseline: 1.1941x; 1.1941x over previous
import math
import numpy as np
import ml_dtypes
from contextlib import ExitStack

import concourse.bass as bass
import concourse.mybir as mybir
from concourse.alu_op_type import AluOpType
from concourse.bass_utils import run_bass_kernel_spmd

F32 = mybir.dt.float32
BF16 = mybir.dt.bfloat16
F16 = mybir.dt.float16
AF = mybir.ActivationFunctionType
AX = mybir.AxisListType

B, S, DM, C = 2, 4096, 4096, 1024
NH, DH, I = 32, 128, 14336
KC = DM // 128
NB = I // 128
NT = S // 128
OT = C // 128
QTN = C // 128
KBN = S // 512
G2 = 8
EPS = 1e-5
QSCALE = float(np.float32(DH ** -0.5))
SENT = -30000.0


def _alibi_slopes(n):
    cp2 = 2 ** math.floor(math.log2(n))
    base = 2.0 ** (-(2.0 ** (-(math.log2(cp2) - 3))))
    slopes = base ** np.arange(1, cp2 + 1, dtype=np.float64)
    if cp2 != n:
        extra_base = 2.0 ** (-(2.0 ** (-(math.log2(2 * cp2) - 3))))
        rem = min(cp2, n - cp2)
        extra = extra_base ** np.arange(1, 1 + 2 * rem, 2, dtype=np.float64)
        slopes = np.concatenate([slopes, extra])
    return slopes.astype(np.float32)


SLOPES = [float(s) for s in _alibi_slopes(NH)]


class Cnt:

    def __init__(self, handle, step):
        self.h = handle
        self.step = step
        self.n = 0

    def bump(self):
        self.n += self.step
        return self.n


class Emit:

    def __init__(self):
        self.streams = {k: [] for k in ("tensor", "vector", "scalar", "gpsimd", "sync")}
        self.waited = {}

    def op(self, eng_name, fn, waits=(), inc=None):
        real_waits = []
        key0 = eng_name
        for cnt, val in waits:
            if val <= 0:
                continue
            k = (key0, id(cnt))
            if self.waited.get(k, -1) >= val:
                continue
            self.waited[k] = val
            real_waits.append((cnt.h, val))
        incv = None
        if inc is not None:
            incv = (inc.h, inc.step)
            inc.n += inc.step

        def run(eng, fn=fn, real_waits=real_waits, incv=incv):
            for h, v in real_waits:
                eng.wait_ge(h, v)
            inst = fn(eng)
            if incv is not None:
                assert inst is not None
                inst.then_inc(incv[0], incv[1])

        self.streams[eng_name].append(run)


def build_program():
    nc = bass.Bass("TRN2", target_bir_lowering=False)

    xb = nc.dram_tensor("xb", [S, DM], F32, kind="ExternalInput").ap()
    xchunk = nc.dram_tensor("xchunk", [C, DM], F32, kind="ExternalInput").ap()
    wq_l = nc.dram_tensor("wq_l", [NH, 128, KC * 128], F32, kind="ExternalInput").ap()
    wkv = nc.dram_tensor("wkv", [DM, 256], F32, kind="ExternalInput").ap()
    wo_l = nc.dram_tensor("wo_l", [8, 128, KC * 512], F32, kind="ExternalInput").ap()
    w1l = nc.dram_tensor("w1l", [NB, 128, KC * 128], F32, kind="ExternalInput").ap()
    w3l = nc.dram_tensor("w3l", [NB, 128, KC * 128], F32, kind="ExternalInput").ap()
    w2t = nc.dram_tensor("w2t", [I, DM], F32, kind="ExternalInput").ap()
    sc1 = nc.dram_tensor("sc1", [128, NB * KC], F32, kind="ExternalInput").ap()
    sc3 = nc.dram_tensor("sc3", [128, NB * KC], F32, kind="ExternalInput").ap()
    sc2 = nc.dram_tensor("sc2", [128, NB * KC], F32, kind="ExternalInput").ap()
    dlt = nc.dram_tensor("dlt", [128, 4096], F16, kind="ExternalInput").ap()
    identb = nc.dram_tensor("identb", [128, 128], BF16, kind="ExternalInput").ap()
    identh = nc.dram_tensor("identh", [128, 128], F16, kind="ExternalInput").ap()
    qscl = nc.dram_tensor("qscl", [128, 1], F32, kind="ExternalInput").ap()

    att_h = nc.dram_tensor("att_h", [DM, C], BF16, kind="Internal").ap()
    s_h = nc.dram_tensor("s_h", [C, DM], BF16, kind="Internal").ap()
    y_h = nc.dram_tensor("y_h", [C, DM], F32, kind="Internal").ap()
    out = nc.dram_tensor("out", [C, DM], F32, kind="ExternalOutput").ap()

    wkv_v = wkv.rearrange("(c p) n -> p c n", p=128)
    w2t_v = w2t.rearrange("(c p) n -> p c n", p=128)
    att_hv = att_h.rearrange("(c p) t -> p c t", p=128)

    em = Emit()

    with ExitStack() as top:
        sem = lambda name, step: Cnt(top.enter_context(nc.semaphore(name)), step)
        qsp = sem("qsp", 16)
        qgp = sem("qgp", 16)
        spe = sem("spe", 1)
        sve = sem("sve", 1)
        sse = sem("sse", 1)
        sge = sem("sge", 1)

        SB = lambda st, name, shape, dt: st.enter_context(nc.sbuf_tensor(name, shape, dt))
        PS = lambda st, name, shape, dt: st.enter_context(nc.psum_tensor(name, shape, dt))

        idb_sb = SB(top, "idb_sb", [128, 128], BF16)
        qscl_sb = SB(top, "qscl_sb", [128, 1], F32)
        idh_sb = SB(top, "idh_sb", [128, 128], F16)
        xnst = ExitStack()
        xnT = SB(xnst, "xnT", [128, KC * 1024], BF16)
        ph01 = ExitStack()
        K_sb = SB(ph01, "K_sb", [128, S], BF16)
        V_sb = SB(ph01, "V_sb", [128, KC * 128], BF16)
        dlt_sb = SB(ph01, "dlt_sb", [128, 4096], F16)

        ph0 = ExitStack()
        xbuf = [SB(ph0, f"xbuf{m}", [128, DM], F32) for m in range(2)]
        sqbuf = SB(ph0, "sqbuf", [128, DM], F32)
        xn_sb = [SB(ph0, f"xn{m}", [128, DM], BF16) for m in range(2)]
        xnt_tmp = [SB(ph0, f"xnt_tmp{m}", [128, KC * 128], BF16) for m in range(2)]
        wkv_sb = SB(ph0, "wkv_sb", [128, KC * 256], BF16)
        stats = {}
        for nm in ("sumx", "sumsq", "negmean", "var", "vareps", "std", "rstd",
                   "negmr"):
            stats[nm] = [SB(ph0, f"{nm}{m}", [128, 1], F32) for m in range(2)]
        tr_ps = [PS(ph0, f"tr_ps{m}", [128, 1024], BF16) for m in range(2)]
        k_ps = [PS(ph0, f"k_ps{m}", [128, 512], F32) for m in range(2)]
        v_ps = [PS(ph0, f"v_ps{m}", [128, 512], F32) for m in range(2)]

        em.op("gpsimd", lambda g: g.dma_start(
            wkv_sb[:].rearrange("p (c n) -> p c n", c=KC), wkv_v),
            inc=qgp)
        q_wkv = qgp.n
        em.op("gpsimd", lambda g: g.dma_start(dlt_sb[:], dlt), inc=qgp)
        em.op("gpsimd", lambda g: g.dma_start(idb_sb[:], identb), inc=qgp)
        em.op("gpsimd", lambda g: g.dma_start(idh_sb[:], identh), inc=qgp)
        em.op("gpsimd", lambda g: g.dma_start(qscl_sb[:], qscl), inc=qgp)
        q_const = qgp.n

        mark = {}

        def phase0_tile(i, src_rows, own):
            m = i % 2
            waits = []
            if mark.get(("xapply", i - 2)) is not None:
                waits.append((sse, mark[("xapply", i - 2)]))
            em.op("sync", lambda sy, sr=src_rows: sy.dma_start(xbuf[m][:], sr),
                  waits=waits, inc=qsp)
            q_x = qsp.n
            waits = [(qsp, q_x)]
            if ("stats", i - 2) in mark:
                waits.append((sve, mark[("stats", i - 2)]))
            em.op("scalar", lambda s: s.activation(
                sqbuf[:], xbuf[m][:], AF.Square, accum_out=stats["sumsq"][m][:]),
                waits=waits, inc=sse)
            a0 = sse.n
            em.op("vector", lambda v: v.tensor_reduce(
                stats["sumx"][m][:], xbuf[m][:], AX.X, AluOpType.add),
                waits=waits, inc=sve)
            g1 = sve.n
            em.op("vector", lambda v: v.tensor_scalar(
                stats["negmean"][m][:], stats["sumx"][m][:], -1.0 / DM, None,
                AluOpType.mult), waits=[(sve, g1)], inc=sve)
            g2 = sve.n
            em.op("vector", lambda v: v.scalar_tensor_tensor(
                stats["var"][m][:], stats["sumx"][m][:], stats["negmean"][m][:],
                stats["sumsq"][m][:], AluOpType.mult, AluOpType.add),
                waits=[(sve, g2), (sse, a0)], inc=sve)
            g3 = sve.n
            em.op("vector", lambda v: v.tensor_scalar(
                stats["vareps"][m][:], stats["var"][m][:], 1.0 / DM, EPS,
                AluOpType.mult, AluOpType.add), waits=[(sve, g3)], inc=sve)
            g3b = sve.n
            em.op("scalar", lambda s: s.activation(
                stats["std"][m][:], stats["vareps"][m][:], AF.Sqrt),
                waits=[(sve, g3b)], inc=sse)
            a1 = sse.n
            em.op("vector", lambda v: v.reciprocal(
                stats["rstd"][m][:], stats["std"][m][:]), waits=[(sse, a1)], inc=sve)
            g4 = sve.n
            em.op("vector", lambda v: v.tensor_scalar(
                stats["negmr"][m][:], stats["sumx"][m][:], stats["rstd"][m][:],
                -1.0 / DM, AluOpType.mult, AluOpType.mult),
                waits=[(sve, g4)], inc=sve)
            g5 = sve.n
            mark[("stats", i)] = g5
            waits = [(sve, g5)]
            if ("tr", i - 2, 3) in mark:
                waits.append((spe, mark[("tr", i - 2, 3)]))
            em.op("scalar", lambda s: s.activation(
                xn_sb[m][:], xbuf[m][:], AF.Identity,
                bias=stats["negmr"][m][:], scale=stats["rstd"][m][:]),
                waits=waits, inc=sse)
            a2 = sse.n
            mark[("xapply", i)] = a2
            for r in range(4):
                mb2 = r % 2
                waits = [(sse, a2), (qgp, q_const)]
                if ("trevac", i, r - 2) in mark:
                    waits.append(mark[("trevac", i, r - 2)])
                elif ("trevac", i - 1, r + 2) in mark:
                    waits.append(mark[("trevac", i - 1, r + 2)])
                for cc in range(8):
                    c = r * 8 + cc
                    em.op("tensor", lambda t, c=c, cc=cc, mb2=mb2, m=m: t.matmul(
                        tr_ps[mb2][:, cc * 128:(cc + 1) * 128],
                        xn_sb[m][:, c * 128:(c + 1) * 128], idb_sb[:],
                        start=(cc == 0), stop=(cc == 7), is_transpose=True),
                        waits=(waits if cc == 0 else ()),
                        inc=(spe if cc == 7 else None))
                mark[("tr", i, r)] = spe.n
                if own:
                    dst = xnT[:].rearrange("p (c t) -> p c t", t=1024)[
                        :, r * 8:(r + 1) * 8, (i % 8) * 128:((i % 8) + 1) * 128]
                else:
                    dst = xnt_tmp[m][:, r * 1024:(r + 1) * 1024].rearrange(
                        "p (c t) -> p c t", t=128)
                em.op("vector", lambda v, dst=dst, mb2=mb2: v.tensor_copy(
                    dst, tr_ps[mb2][:].rearrange("p (c t) -> p c t", t=128)),
                    waits=[(spe, mark[("tr", i, r)])], inc=sve)
                mark[("trevac", i, r)] = (sve, sve.n)
            if own:
                return
            waits = [mark[("trevac", i, 3)], (qgp, q_wkv)]
            if ("kve", i - 2) in mark:
                waits.append((sve, mark[("kve", i - 2)]))
            for c in range(KC):
                src_c = xnt_tmp[m][:, c * 128:(c + 1) * 128]
                em.op("tensor", lambda t, c=c, src_c=src_c: t.matmul(
                    k_ps[m][:, 0:128], wkv_sb[:, c * 256:c * 256 + 128], src_c,
                    start=(c == 0), stop=(c == KC - 1)),
                    waits=(waits if c == 0 else ()),
                    inc=(spe if c == KC - 1 else None))
            for c in range(KC):
                src_c = xnt_tmp[m][:, c * 128:(c + 1) * 128]
                em.op("tensor", lambda t, c=c, src_c=src_c: t.matmul(
                    v_ps[m][:, 0:128], src_c, wkv_sb[:, c * 256 + 128:c * 256 + 256],
                    start=(c == 0), stop=(c == KC - 1)),
                    inc=(spe if c == KC - 1 else None))
            mark[("kv", i)] = spe.n
            em.op("vector", lambda v: v.tensor_copy(
                K_sb[:, i * 128:(i + 1) * 128], k_ps[m][:, 0:128]),
                waits=[(spe, mark[("kv", i)])])
            em.op("vector", lambda v: v.tensor_copy(
                V_sb[:, i * 128:(i + 1) * 128], v_ps[m][:, 0:128]), inc=sve)
            mark[("kve", i)] = sve.n

        for i in range(NT):
            phase0_tile(i, xb[i * 128:(i + 1) * 128, :], own=False)
        for i in range(NT, NT + OT):
            phase0_tile(i, xchunk[(i - NT) * 128:(i - NT + 1) * 128, :], own=True)

        ph0_done_se = sse.n
        ph0_done_ve = sve.n
        ph0_done_pe = spe.n
        ph0.close()

        ph1 = ExitStack()
        wq_sb = [SB(ph1, f"wq{m}", [128, KC * 128], BF16) for m in range(2)]
        qT_sb = [SB(ph1, f"qT{m}", [128, C], BF16) for m in range(2)]
        t_sb = [SB(ph1, f"t{m}", [128, S], F16) for m in range(2)]
        p_sb = [SB(ph1, f"pp{m}", [128, S], F16) for m in range(2)]
        ptA = SB(ph1, "ptA", [128, 16 * 512], BF16)
        ptB = SB(ph1, "ptB", [128, 32 * 512], BF16)
        negmx = [SB(ph1, f"negmx{m}", [128, 1], F32) for m in range(2)]
        mx8 = [SB(ph1, f"mx8{m}", [128, 8], F32) for m in range(2)]
        den = [SB(ph1, f"den{m}", [128, 1], F32) for m in range(2)]
        rec = [SB(ph1, f"rec{m}", [128, 1], F32) for m in range(2)]
        d_sb = [SB(ph1, f"dd{m}", [128, 128], F16) for m in range(2)]
        att_st = [SB(ph1, f"attst{m}", [128, C], BF16) for m in range(2)]
        q_ps = [PS(ph1, f"q_ps{m}", [128, 512], F32) for m in range(2)]
        sc_ps = [PS(ph1, f"sc_ps{m}", [128, 512], F32) for m in range(2)]
        pt_ps = [PS(ph1, f"pt_ps{m}", [128, 512], F32) for m in range(2)]
        av_ps = [PS(ph1, f"av_ps{m}", [128, 512], F32) for m in range(2)]

        em.op("vector", lambda v: v.memset(ptA[:], 0.0),
              waits=[(sse, ph0_done_se), (sve, ph0_done_ve), (spe, ph0_done_pe)])
        em.op("vector", lambda v: v.memset(ptB[:], 0.0), inc=sve)
        pt_zeroed = sve.n

        def emit_head(h):
            m = h % 2
            if h < 2:
                waits = [(sse, ph0_done_se), (sve, ph0_done_ve),
                         (spe, ph0_done_pe)]
            else:
                waits = [(spe, mark[("qp", h - 2)])]
            em.op("gpsimd", lambda g: g.dma_start(
                wq_sb[m][:], wq_l[h]), waits=waits, inc=qgp)
            mark[("wq", h)] = qgp.n

        def emit_qp_chunk(h, c0, c1):
            m = h % 2
            waits = []
            if c0 == 0:
                waits = [(qgp, mark[("wq", h)]), (sse, ph0_done_se),
                         (sve, ph0_done_ve)]
                if ("qe", h - 1) in mark:
                    waits.append(mark[("qe", h - 1)])
            first = True
            for c in range(c0, c1):
                for half in range(2):
                    em.op("tensor", lambda t, half=half, c=c, m=m: t.matmul(
                        q_ps[half][:],
                        wq_sb[m][:, c * 128:(c + 1) * 128],
                        xnT[:, c * 1024 + half * 512: c * 1024 + half * 512 + 512],
                        start=(c == 0), stop=(c == KC - 1)),
                        waits=(waits if first else ()),
                        inc=(spe if (c == c1 - 1 and half == 1) else None))
                    first = False
            if c1 == KC:
                mark[("qp", h)] = spe.n

        def emit_qe(h):
            m = h % 2
            for half in range(2):
                em.op("scalar", lambda s, half=half, m=m: s.activation(
                    qT_sb[m][:, half * 512:(half + 1) * 512], q_ps[half][:],
                    AF.Identity, scale=qscl_sb[:]),
                    waits=([(spe, mark[("qp", h)]), (qgp, q_const)]
                           if half == 0 else ()),
                    inc=(sse if half == 1 else None))
            mark[("qe", h)] = (sse, sse.n)

        def emit_head_body(h):
            m = h % 2
            slope = SLOPES[h]
            if h == 0:
                emit_qe(0)
            def emit_sc(qt):
                for kb in range(qt + 1):
                    emit_sc_kb(qt, kb)
                emit_mx(qt)

            def emit_sc_kb(qt, kb):
                mq = qt % 2
                if True:
                    bank = len(seq["stt"]) % 2
                    waits = [mark[("qe", h)], (sve, pt_zeroed)]
                    if len(seq["stt"]) >= 2:
                        waits.append((sve, seq["stt"][-2]))
                    em.op("tensor", lambda t, qt=qt, kb=kb, bank=bank: t.matmul(
                        sc_ps[bank][:],
                        qT_sb[m][:, qt * 128:(qt + 1) * 128],
                        K_sb[:, kb * 512:(kb + 1) * 512],
                        start=True, stop=True),
                        waits=waits, inc=spe)
                    mark[("sc", h, qt, kb)] = spe.n
                    off = (kb - qt) * 512 + 3584
                    waits = [(spe, mark[("sc", h, qt, kb)])]
                    if kb == 0:
                        pe_exp = mark.get(("exp", h, qt - 2)) or mark.get(
                            ("exp", h - 1, qt + QTN - 2))
                        if pe_exp:
                            waits.append((sse, pe_exp))
                    em.op("vector", lambda v, qt=qt, kb=kb, off=off, mq=mq,
                          bank=bank: v.scalar_tensor_tensor(
                              t_sb[mq][:, kb * 512:(kb + 1) * 512],
                              dlt_sb[:, off:off + 512], slope, sc_ps[bank][:],
                              AluOpType.mult, AluOpType.add),
                          waits=waits, inc=sve)
                    seq["stt"].append(sve.n)
                    em.op("vector", lambda v, qt=qt, kb=kb, mq=mq: v.tensor_reduce(
                        mx8[mq][:, kb:kb + 1],
                        t_sb[mq][:, kb * 512:(kb + 1) * 512],
                        AX.X, AluOpType.max),
                        waits=[(sve, seq["stt"][-1])], inc=sve)
                    mark[("mxp", h, qt, kb)] = sve.n

            def emit_mx(qt):
                mq = qt % 2
                em.op("vector", lambda v, mq=mq, qt=qt: v.tensor_reduce(
                    negmx[mq][:], mx8[mq][:, 0:qt + 1], AX.X,
                    AluOpType.max, negate=True),
                    waits=[(sve, mark[("mxp", h, qt, qt)])], inc=sve)
                mark[("mx", h, qt)] = sve.n
                waits = [(sve, mark[("mx", h, qt)])]
                pe_pt = mark.get(("ptdone", h, qt - 2)) or mark.get(
                    ("ptdone", h - 1, qt + QTN - 2))
                if pe_pt:
                    waits.append((spe, pe_pt))
                em.op("scalar", lambda s, mq=mq, qt=qt: s.activation(
                    p_sb[mq][:, 0:(qt + 1) * 512], t_sb[mq][:, 0:(qt + 1) * 512],
                    AF.Exp, bias=negmx[mq][:], accum_out=den[mq][:]),
                    waits=waits, inc=sse)
                mark[("exp", h, qt)] = sse.n

            def emit_rd(qt):
                mq = qt % 2
                em.op("vector", lambda v, mq=mq: v.reciprocal(
                    rec[mq][:], den[mq][:]),
                    waits=[(sse, mark[("exp", h, qt)])], inc=sve)
                rr = sve.n
                waits = [(sve, rr)]
                pe_pt = mark.get(("ptdone", h, qt - 2)) or mark.get(
                    ("ptdone", h - 1, qt + QTN - 2))
                if pe_pt:
                    waits.append((spe, pe_pt))
                em.op("vector", lambda v, mq=mq: v.tensor_scalar(
                    d_sb[mq][:], idh_sb[:], rec[mq][:], None, AluOpType.mult),
                    waits=waits, inc=sve)
                mark[("dd", h, qt)] = sve.n

            def emit_pt_one(qt, g):
                mq = qt % 2
                qg = qt // 4
                dstbuf = ptA if qg == 0 else ptB
                if True:
                    bank = len(seq["pte"]) % 2
                    waits = [(sse, mark[("exp", h, qt)]), (sve, mark[("dd", h, qt)])]
                    if len(seq["pte"]) >= 2:
                        waits.append((sse, seq["pte"][-2]))
                    for kk in range(4):
                        ks = g * 4 + kk
                        em.op("tensor", lambda t, ks=ks, kk=kk, mq=mq,
                              bank=bank: t.matmul(
                            pt_ps[bank][:, kk * 128:(kk + 1) * 128],
                            p_sb[mq][:, ks * 128:(ks + 1) * 128], d_sb[mq][:],
                            start=(kk == 0), stop=(kk == 3)),
                            waits=(waits if kk == 0 else ()),
                            inc=(spe if kk == 3 else None))
                    mark[("ptg", h, qt, g)] = spe.n
                    dst = dstbuf[:].rearrange("p (c q) -> p c q", q=512)[
                        :, g * 4:(g + 1) * 4, (qt % 4) * 128:((qt % 4) + 1) * 128]
                    waits = [(spe, mark[("ptg", h, qt, g)])]
                    if qt % 4 == 0 and g == 0:
                        prev_av = mark.get(("av", h, qt // 4 - 1)) or mark.get(
                            ("av", h - 1, qt // 4))
                        if prev_av:
                            waits.append((spe, prev_av))
                    em.op("scalar", lambda s, dst=dst, bank=bank: s.copy(
                        dst, pt_ps[bank][:].rearrange("p (c q) -> p c q", q=128)),
                        waits=waits, inc=sse)
                    seq["pte"].append(sse.n)
                if g == qt:
                    mark[("ptdone", h, qt)] = spe.n
                    mark[("ptedone", h, qt)] = sse.n

            def emit_av(qg):
                srcbuf = ptA if qg == 0 else ptB
                ncks = 16 * qg + 16
                waits = [(sse, mark[("ptedone", h, qg * 4 + 3)])]
                prev = mark.get(("ave", h, qg - 2)) or mark.get(("ave", h - 1, qg))
                if prev:
                    waits.append((sse, prev))
                for ck in range(ncks):
                    em.op("tensor", lambda t, ck=ck, qg=qg, srcbuf=srcbuf: t.matmul(
                        av_ps[qg % 2][:],
                        V_sb[:, ck * 128:(ck + 1) * 128],
                        srcbuf[:, ck * 512:(ck + 1) * 512],
                        start=(ck == 0), stop=(ck == ncks - 1)),
                        waits=(waits if ck == 0 else ()),
                        inc=(spe if ck == ncks - 1 else None))
                mark[("av", h, qg)] = spe.n
                waits = [(spe, mark[("av", h, qg)])]
                if ("att_store", h - 2) in mark:
                    waits.append((qsp, mark[("att_store", h - 2)]))
                em.op("scalar", lambda s, qg=qg: s.copy(
                    att_st[m][:, qg * 512:(qg + 1) * 512], av_ps[qg % 2][:]),
                    waits=waits, inc=sse)
                mark[("ave", h, qg)] = sse.n

            emit_sc(0)
            for qt in range(1, QTN):
                ptg_next = 0
                for kb in range(qt + 1):
                    emit_sc_kb(qt, kb)
                    if kb == min(1, qt):
                        emit_rd(qt - 1)
                    if kb >= 3 and ptg_next < qt:
                        emit_pt_one(qt - 1, ptg_next)
                        ptg_next += 1
                while ptg_next < qt:
                    emit_pt_one(qt - 1, ptg_next)
                    ptg_next += 1
                emit_mx(qt)
                if qt - 1 == 3:
                    emit_av(0)
                if qt >= 4 and h + 1 < NH:
                    emit_qp_chunk(h + 1, (qt - 4) * 8, (qt - 3) * 8)
                    if qt == QTN - 1:
                        emit_qe(h + 1)
            emit_rd(QTN - 1)
            for g in range(QTN):
                emit_pt_one(QTN - 1, g)
            emit_av(1)
            em.op("sync", lambda sy: sy.dma_start(
                att_h[h * 128:(h + 1) * 128, :], att_st[m][:]),
                waits=[(sse, mark[("ave", h, 1)])], inc=qsp)
            mark[("att_store", h)] = qsp.n

        seq = {"stt": [], "pte": []}
        emit_head(0)
        emit_head(1)
        emit_qp_chunk(0, 0, KC)
        for h in range(NH):
            if h + 2 < NH:
                emit_head(h + 2)
            emit_head_body(h)

        ph1_stores = qsp.n
        ph1.close()
        ph01.close()

        ph2 = ExitStack()
        att_all = SB(ph2, "att_all", [128, KC * 1024], BF16)
        wo_sb = [SB(ph2, f"wo{m}", [128, KC * 512], BF16) for m in range(2)]
        s_st = [SB(ph2, f"s_st{m}", [128, 512], BF16) for m in range(2)]
        s_ps = [PS(ph2, f"s_ps{m}", [128, 512], F32) for m in range(2)]

        em.op("sync", lambda sy: sy.dma_start(
            att_all[:].rearrange("p (c t) -> p c t", t=1024), att_hv),
            waits=[(qsp, ph1_stores)], inc=qsp)
        q_attall = qsp.n

        def emit_wo_load(dt):
            mw = dt % 2
            if dt < 2:
                waits = [(qsp, ph1_stores)]
            else:
                waits = [(spe, mark[("wopd", dt - 2)])]
            em.op("gpsimd", lambda g, dt=dt, mw=mw: g.dma_start(
                wo_sb[mw][:], wo_l[dt]), waits=waits, inc=qgp)
            mark[("wol", dt)] = qgp.n

        emit_wo_load(0)
        emit_wo_load(1)
        for dt in range(8):
            mw = dt % 2
            for qt in range(QTN):
                waits = [(qgp, mark[("wol", dt)]), (qsp, q_attall)]
                prev = mark.get(("se2", dt, qt - 2)) or mark.get(("se2", dt - 1, qt + 6))
                if prev:
                    waits.append((sse, prev))
                for hc in range(KC):
                    em.op("tensor", lambda t, dt=dt, qt=qt, hc=hc, mw=mw: t.matmul(
                        s_ps[qt % 2][:],
                        att_all[:, hc * 1024 + qt * 128: hc * 1024 + qt * 128 + 128],
                        wo_sb[mw][:, hc * 512:(hc + 1) * 512],
                        start=(hc == 0), stop=(hc == KC - 1)),
                        waits=(waits if hc == 0 else ()),
                        inc=(spe if hc == KC - 1 else None))
                mark[("wopq", dt, qt)] = spe.n
                waits = [(spe, mark[("wopq", dt, qt)])]
                prev = mark.get(("s_store", dt, qt - 2)) or mark.get(
                    ("s_store", dt - 1, qt + 6))
                if prev:
                    waits.append((qsp, prev))
                em.op("scalar", lambda s, qt=qt: s.copy(
                    s_st[qt % 2][:], s_ps[qt % 2][:]), waits=waits, inc=sse)
                mark[("se2", dt, qt)] = sse.n
                em.op("sync", lambda sy, dt=dt, qt=qt: sy.dma_start(
                    s_h[qt * 128:(qt + 1) * 128, dt * 512:(dt + 1) * 512],
                    s_st[qt % 2][:]),
                    waits=[(sse, mark[("se2", dt, qt)])], inc=qsp)
                mark[("s_store", dt, qt)] = qsp.n
            mark[("wopd", dt)] = spe.n
            if dt + 2 < 8:
                emit_wo_load(dt + 2)

        s_stores = qsp.n
        ph2.close()

        ph3 = ExitStack()
        scs_sb = [SB(ph3, f"scs{m}", [128, 3 * KC], F32) for m in range(2)]
        w1r = [SB(ph3, f"w1r{m}", [128, KC * 128], BF16) for m in range(2)]
        w3r = [SB(ph3, f"w3r{m}", [128, KC * 128], BF16) for m in range(2)]
        w2g = SB(ph3, "w2g", [128, G2 * DM], BF16)
        hT = SB(ph3, "hT", [128, G2 * 1024], BF16)
        gs_tmp = SB(ph3, "gs_tmp", [128, 1024], F32)
        y_st = [SB(ph3, f"y_st{m}", [128, 2048], F32) for m in range(2)]
        g_ps = PS(ph3, "g_ps", [128, 1024], F32)
        u_ps = PS(ph3, "u_ps", [128, 1024], F32)
        y_ps = [PS(ph3, f"y_ps{m}", [128, 512], F32) for m in range(2)]
        y_ps2 = [PS(ph3, f"y_ps2{m}", [128, 512], F32) for m in range(2)]

        def dq_fn(v, buf, scoff, msc, nb):
            t3 = buf.rearrange("p (c n) -> p c n", n=128)
            scb = scs_sb[msc][:, scoff * KC:(scoff + 1) * KC].unsqueeze(
                2).broadcast_to([128, KC, 128])
            return v.tensor_tensor(t3, t3, scb, AluOpType.mult)

        def emit_w13(nb):
            m = nb % 2
            waits = []
            if ("f1h", nb - 2, 1) in mark:
                waits.append((spe, mark[("f1h", nb - 2, 1)]))
            else:
                waits.append((qsp, s_stores))
            swaits = list(waits)
            if ("dq2", nb - 2) in mark:
                swaits.append((sve, mark[("dq2", nb - 2)]))
            em.op("sync", lambda sy, nb=nb, m=m: sy.dma_start(
                scs_sb[m][:, 0:KC], sc1[:, nb * KC:(nb + 1) * KC]),
                waits=swaits, inc=qsp)
            em.op("sync", lambda sy, nb=nb, m=m: sy.dma_start(
                scs_sb[m][:, KC:2 * KC], sc3[:, nb * KC:(nb + 1) * KC]), inc=qsp)
            em.op("sync", lambda sy, nb=nb, m=m: sy.dma_start(
                scs_sb[m][:, 2 * KC:3 * KC], sc2[:, nb * KC:(nb + 1) * KC]),
                inc=qsp)
            mark[("scl", nb)] = qsp.n
            em.op("gpsimd", lambda g, nb=nb, m=m: g.dma_start(
                w1r[m][:], w1l[nb]), waits=waits, inc=qgp)
            em.op("gpsimd", lambda g, nb=nb, m=m: g.dma_start(
                w3r[m][:], w3l[nb]), inc=qgp)
            q_w = qgp.n
            em.op("vector", lambda v, m=m, nb=nb: dq_fn(v, w1r[m][:], 0, m, nb),
                  waits=[(qgp, q_w), (qsp, mark[("scl", nb)])])
            em.op("vector", lambda v, m=m, nb=nb: dq_fn(v, w3r[m][:], 1, m, nb),
                  inc=sve)
            mark[("dq13", nb)] = sve.n

        def emit_w2(nb):
            gidx = nb // G2
            slot = nb % G2
            m = nb % 2
            waits = []
            if ("f2", gidx - 1) in mark:
                waits.append((spe, mark[("f2", gidx - 1)]))
            elif gidx == 0:
                waits.append((qsp, s_stores))
            em.op("gpsimd", lambda g, nb=nb, slot=slot: g.dma_start(
                w2g[:, slot * DM:(slot + 1) * DM], w2t_v[:, nb, :]),
                waits=waits, inc=qgp)
            q_w = qgp.n
            em.op("vector", lambda v, slot=slot, nb=nb, m=m: dq_fn(
                v, w2g[:, slot * DM:(slot + 1) * DM], 2, m, nb),
                waits=[(qgp, q_w), (qsp, mark[("scl", nb)])], inc=sve)
            mark[("dq2", nb)] = sve.n

        emit_w13(0)
        emit_w2(0)
        for nb in range(NB):
            m = nb % 2
            gidx = nb // G2
            slot = nb % G2
            if nb + 1 < NB:
                emit_w13(nb + 1)
            for half in range(2):
                waits = [(sve, mark[("dq13", nb)]), (sse, ph0_done_se),
                         (sve, ph0_done_ve)]
                if ("hm", nb - 1, half) in mark:
                    waits.append((sve, mark[("hm", nb - 1, half)]))
                if slot == 0 and half == 0 and ("ye", gidx - 1, 7, 7) in mark:
                    waits.append((sse, mark[("ye", gidx - 1, 7, 7)]))
                first = True
                for c in range(KC):
                    em.op("tensor", lambda t, c=c, half=half, m=m: t.matmul(
                        g_ps[:, half * 512:(half + 1) * 512],
                        w1r[m][:, c * 128:(c + 1) * 128],
                        xnT[:, c * 1024 + half * 512: c * 1024 + half * 512 + 512],
                        start=(c == 0), stop=(c == KC - 1)),
                        waits=(waits if first else ())); first = False
                    em.op("tensor", lambda t, c=c, half=half, m=m: t.matmul(
                        u_ps[:, half * 512:(half + 1) * 512],
                        w3r[m][:, c * 128:(c + 1) * 128],
                        xnT[:, c * 1024 + half * 512: c * 1024 + half * 512 + 512],
                        start=(c == 0), stop=(c == KC - 1)),
                        inc=(spe if c == KC - 1 else None))
                mark[("f1h", nb, half)] = spe.n
                em.op("scalar", lambda s, half=half: s.activation(
                    gs_tmp[:, half * 512:(half + 1) * 512],
                    g_ps[:, half * 512:(half + 1) * 512], AF.Silu),
                    waits=[(spe, mark[("f1h", nb, half)])], inc=sse)
                mark[("si", nb, half)] = sse.n
                em.op("vector", lambda v, slot=slot, half=half: v.tensor_tensor(
                    hT[:, slot * 1024 + half * 512: slot * 1024 + half * 512 + 512],
                    gs_tmp[:, half * 512:(half + 1) * 512],
                    u_ps[:, half * 512:(half + 1) * 512],
                    AluOpType.mult), waits=[(sse, mark[("si", nb, half)])],
                    inc=sve)
                mark[("hm", nb, half)] = sve.n
            if slot == G2 - 1:
                g = gidx
                ybank = [g_ps[:, 0:512], g_ps[:, 512:1024],
                         u_ps[:, 0:512], u_ps[:, 512:1024],
                         y_ps[0][:], y_ps[1][:], y_ps2[0][:], y_ps2[1][:]]
                for tb in range(8):
                    for nn in range(G2):
                        for dt in range(8):
                            waits = []
                            if nn == 0:
                                waits = [(sve, mark[("hm", nb, 1)]),
                                         (sve, mark[("dq2", nb)])]
                                ye_prev = mark.get(("ye", g, tb - 1, dt)) or \
                                    mark.get(("ye", g - 1, 7, dt))
                                if ye_prev:
                                    waits.append((sse, ye_prev))
                            em.op("tensor", lambda t, tb=tb, dt=dt, nn=nn: t.matmul(
                                ybank[dt],
                                hT[:, nn * 1024 + tb * 128: nn * 1024 + tb * 128 + 128],
                                w2g[:, nn * DM + dt * 512: nn * DM + (dt + 1) * 512],
                                start=(nn == 0), stop=(nn == G2 - 1)),
                                waits=waits,
                                inc=(spe if (nn == G2 - 1 and dt == 7) else None))
                    mark[("f2t", g, tb)] = spe.n
                    for dt in range(8):
                        half = dt // 4
                        ybuf = (tb * 2 + half) % 2
                        waits = [(spe, mark[("f2t", g, tb)])]
                        if dt % 4 == 0:
                            prev = mark.get(("y_store", g, tb - 1, half)) or \
                                mark.get(("y_store", g - 1, 7, half))
                            if prev:
                                waits.append((qgp, prev))
                        em.op("scalar", lambda s, dt=dt, ybuf=ybuf: s.copy(
                            y_st[ybuf][:, (dt % 4) * 512:((dt % 4) + 1) * 512],
                            ybank[dt]), waits=waits, inc=sse)
                        mark[("ye", g, tb, dt)] = sse.n
                        if dt % 4 == 3:
                            em.op("gpsimd", lambda gp, tb=tb, g=g, half=half,
                                  ybuf=ybuf: gp.dma_start(
                                      y_h[tb * 128:(tb + 1) * 128,
                                          half * 2048:(half + 1) * 2048],
                                      y_st[ybuf][:],
                                      **({"accum_op": AluOpType.add}
                                         if g > 0 else {})),
                                  waits=[(sse, mark[("ye", g, tb, dt)])], inc=qgp)
                            mark[("y_store", g, tb, half)] = qgp.n
                mark[("f2", g)] = spe.n
            if nb + 1 < NB:
                emit_w2(nb + 1)

        y_stores_gp = qgp.n
        ph3.close()
        xnst.close()

        ph4 = ExitStack()
        y_in = [SB(ph4, f"y_in{m}", [128, DM], F32) for m in range(2)]
        si_in = [SB(ph4, f"si_in{m}", [128, DM], BF16) for m in range(2)]
        x_in = [SB(ph4, f"x_in{m}", [128, DM], F32) for m in range(2)]
        gf_tmp = [SB(ph4, f"gf{m}", [128, DM], F32) for m in range(2)]
        sg_tmp = [SB(ph4, f"sg{m}", [128, DM], F32) for m in range(2)]
        out_t = [SB(ph4, f"out_t{m}", [128, DM], F32) for m in range(2)]

        for tb in range(OT):
            m = tb % 2
            waits = [(qgp, mark[("y_store", NB // G2 - 1, tb, 1)]),
                     (qsp, s_stores)]
            if ("cmb", tb - 2) in mark:
                waits.append((sve, mark[("cmb", tb - 2)]))
            em.op("sync", lambda sy, tb=tb, m=m: sy.dma_start(
                y_in[m][:], y_h[tb * 128:(tb + 1) * 128, :]), waits=waits, inc=qsp)
            em.op("sync", lambda sy, tb=tb, m=m: sy.dma_start(
                si_in[m][:], s_h[tb * 128:(tb + 1) * 128, :]), inc=qsp)
            em.op("sync", lambda sy, tb=tb, m=m: sy.dma_start(
                x_in[m][:], xchunk[tb * 128:(tb + 1) * 128, :]), inc=qsp)
            q_c = qsp.n
            waits = [(qsp, q_c)]
            if ("cmb_o", tb - 2) in mark:
                waits.append((qsp, mark[("cmb_o", tb - 2)]))
            em.op("scalar", lambda s, m=m: s.activation(
                gf_tmp[m][:], y_in[m][:], AF.Silu), waits=waits, inc=sse)
            c1 = sse.n
            em.op("vector", lambda v, m=m: v.tensor_tensor(
                sg_tmp[m][:], si_in[m][:], gf_tmp[m][:], AluOpType.mult),
                waits=[(sse, c1)], inc=sve)
            c2 = sve.n
            em.op("vector", lambda v, m=m: v.tensor_tensor(
                out_t[m][:], sg_tmp[m][:], x_in[m][:], AluOpType.add),
                waits=[(sve, c2)], inc=sve)
            mark[("cmb", tb)] = sve.n
            em.op("sync", lambda sy, tb=tb, m=m: sy.dma_start(
                out[tb * 128:(tb + 1) * 128, :], out_t[m][:]),
                waits=[(sve, mark[("cmb", tb)])], inc=qsp)
            mark[("cmb_o", tb)] = qsp.n
        ph4.close()

        total_qsp = qsp.n
        total_qgp = qgp.n

        def final_wait(eng):
            eng.wait_ge(qsp.h, total_qsp)
            eng.wait_ge(qgp.h, total_qgp)
        em.streams["sync"].append(final_wait)

        with nc.Block() as blk:
            @blk.tensor
            def _(eng):
                for fn in em.streams["tensor"]:
                    fn(eng)

            @blk.vector
            def _(eng):
                for fn in em.streams["vector"]:
                    fn(eng)

            @blk.scalar
            def _(eng):
                for fn in em.streams["scalar"]:
                    fn(eng)

            @blk.gpsimd
            def _(eng):
                for fn in em.streams["gpsimd"]:
                    fn(eng)

            @blk.sync
            def _(eng):
                for fn in em.streams["sync"]:
                    fn(eng)

    return nc


def _host_inputs(L0_attn_hidden_states, L0_attn_Wqkv, L0_attn_Wo,
                 L0_ffn_w1, L0_ffn_w1_scale, L0_ffn_w3, L0_ffn_w3_scale,
                 L0_ffn_w2, L0_ffn_w2_scale):
    x = np.ascontiguousarray(np.asarray(L0_attn_hidden_states, dtype=np.float32))
    wqkv = np.asarray(L0_attn_Wqkv, dtype=np.float32)
    wo = np.asarray(L0_attn_Wo, dtype=np.float32)

    def tile_kn(wt, nblocks):
        K, N = wt.shape
        return np.ascontiguousarray(
            wt.reshape(K // 128, 128, nblocks, N // nblocks)
            .transpose(2, 1, 0, 3).reshape(nblocks, 128, -1))

    wq_l = tile_kn(wqkv[:, :4096], NH)
    wkv = np.ascontiguousarray(wqkv[:, 4096:])
    wo_l = tile_kn(wo, 8)
    w1l = tile_kn(np.asarray(L0_ffn_w1, dtype=np.float32).T, NB)
    w3l = tile_kn(np.asarray(L0_ffn_w3, dtype=np.float32).T, NB)
    w2t = np.ascontiguousarray(np.asarray(L0_ffn_w2, dtype=np.float32).T)
    s1 = np.asarray(L0_ffn_w1_scale, dtype=np.float32)
    s3 = np.asarray(L0_ffn_w3_scale, dtype=np.float32)
    s2 = np.asarray(L0_ffn_w2_scale, dtype=np.float32)
    sc1 = np.ascontiguousarray(np.broadcast_to(s1.reshape(1, -1), (128, NB * KC)))
    sc3 = np.ascontiguousarray(np.broadcast_to(s3.reshape(1, -1), (128, NB * KC)))
    sc2 = np.ascontiguousarray(np.broadcast_to(
        np.ascontiguousarray(s2.T).reshape(1, -1), (128, NB * KC)))
    identb = np.eye(128, dtype=ml_dtypes.bfloat16)
    identh = np.eye(128, dtype=np.float16)

    in_maps = []
    for core in range(8):
        b, j = core // 4, core % 4
        rows = np.concatenate([np.arange(512 * k + 128 * j, 512 * k + 128 * j + 128)
                               for k in range(QTN)])
        cc = np.arange(4096, dtype=np.float64)[None, :]
        pq = np.arange(128, dtype=np.float64)[:, None]
        v = (cc - 3584.0 - 128.0 * j) - pq
        tmpl = np.where(v <= 0, v, SENT).astype(np.float16)
        in_maps.append(dict(
            xb=np.ascontiguousarray(x[b]),
            xchunk=np.ascontiguousarray(x[b][rows]),
            wq_l=wq_l, wkv=wkv, wo_l=wo_l, w1l=w1l, w3l=w3l, w2t=w2t,
            sc1=sc1, sc3=sc3, sc2=sc2,
            dlt=tmpl, identb=identb, identh=identh,
            qscl=np.full((128, 1), QSCALE, dtype=np.float32),
        ))
    return in_maps


def kernel(L0_attn_hidden_states, L0_attn_Wqkv, L0_attn_Wo,
           L0_ffn_w1, L0_ffn_w1_scale, L0_ffn_w3, L0_ffn_w3_scale,
           L0_ffn_w2, L0_ffn_w2_scale):
    in_maps = _host_inputs(L0_attn_hidden_states, L0_attn_Wqkv, L0_attn_Wo,
                           L0_ffn_w1, L0_ffn_w1_scale, L0_ffn_w3, L0_ffn_w3_scale,
                           L0_ffn_w2, L0_ffn_w2_scale)
    nc = build_program()
    res = run_bass_kernel_spmd(nc, in_maps, core_ids=list(range(8)))
    out = np.empty((B, S, DM), dtype=np.float32)
    for core in range(8):
        b, j = core // 4, core % 4
        rows = np.concatenate([np.arange(512 * k + 128 * j, 512 * k + 128 * j + 128)
                               for k in range(QTN)])
        out[b][rows] = np.asarray(res.results[core]["out"], dtype=np.float32)
    return out


# version 17 (final)
# speedup vs baseline: 1.2390x; 1.0377x over previous
import math
import numpy as np
import ml_dtypes
from contextlib import ExitStack

import concourse.bass as bass
import concourse.mybir as mybir
from concourse.alu_op_type import AluOpType
from concourse.bass_utils import run_bass_kernel_spmd

F32 = mybir.dt.float32
BF16 = mybir.dt.bfloat16
F16 = mybir.dt.float16
AF = mybir.ActivationFunctionType
AX = mybir.AxisListType

B, S, DM, C = 2, 4096, 4096, 1024
NH, DH, I = 32, 128, 14336
KC = DM // 128
NB = I // 128
NT = S // 128
OT = C // 128
QTN = C // 128
KBN = S // 512
G2 = 8
EPS = 1e-5
QSCALE = float(np.float32(DH ** -0.5))
SENT = -30000.0


def _alibi_slopes(n):
    cp2 = 2 ** math.floor(math.log2(n))
    base = 2.0 ** (-(2.0 ** (-(math.log2(cp2) - 3))))
    slopes = base ** np.arange(1, cp2 + 1, dtype=np.float64)
    if cp2 != n:
        extra_base = 2.0 ** (-(2.0 ** (-(math.log2(2 * cp2) - 3))))
        rem = min(cp2, n - cp2)
        extra = extra_base ** np.arange(1, 1 + 2 * rem, 2, dtype=np.float64)
        slopes = np.concatenate([slopes, extra])
    return slopes.astype(np.float32)


SLOPES = [float(s) for s in _alibi_slopes(NH)]


class Cnt:

    def __init__(self, handle, step):
        self.h = handle
        self.step = step
        self.n = 0

    def bump(self):
        self.n += self.step
        return self.n


class Emit:

    def __init__(self):
        self.streams = {k: [] for k in ("tensor", "vector", "scalar", "gpsimd", "sync")}
        self.waited = {}

    def op(self, eng_name, fn, waits=(), inc=None):
        real_waits = []
        key0 = eng_name
        for cnt, val in waits:
            if val <= 0:
                continue
            k = (key0, id(cnt))
            if self.waited.get(k, -1) >= val:
                continue
            self.waited[k] = val
            real_waits.append((cnt.h, val))
        incv = None
        if inc is not None:
            incv = (inc.h, inc.step)
            inc.n += inc.step

        def run(eng, fn=fn, real_waits=real_waits, incv=incv):
            for h, v in real_waits:
                eng.wait_ge(h, v)
            inst = fn(eng)
            if incv is not None:
                assert inst is not None
                inst.then_inc(incv[0], incv[1])

        self.streams[eng_name].append(run)


def build_program():
    nc = bass.Bass("TRN2", target_bir_lowering=False)

    xb = nc.dram_tensor("xb", [S, DM], F32, kind="ExternalInput").ap()
    xchunk = nc.dram_tensor("xchunk", [C, DM], F32, kind="ExternalInput").ap()
    wq_l = nc.dram_tensor("wq_l", [NH, 128, KC * 128], F32, kind="ExternalInput").ap()
    wkv = nc.dram_tensor("wkv", [DM, 256], F32, kind="ExternalInput").ap()
    wo_l = nc.dram_tensor("wo_l", [8, 128, KC * 512], F32, kind="ExternalInput").ap()
    w1l = nc.dram_tensor("w1l", [NB, 128, KC * 128], F32, kind="ExternalInput").ap()
    w3l = nc.dram_tensor("w3l", [NB, 128, KC * 128], F32, kind="ExternalInput").ap()
    w2t = nc.dram_tensor("w2t", [I, DM], F32, kind="ExternalInput").ap()
    sc1 = nc.dram_tensor("sc1", [128, NB * KC], F32, kind="ExternalInput").ap()
    sc3 = nc.dram_tensor("sc3", [128, NB * KC], F32, kind="ExternalInput").ap()
    sc2 = nc.dram_tensor("sc2", [128, NB * KC], F32, kind="ExternalInput").ap()
    dlt = nc.dram_tensor("dlt", [128, 4096], F16, kind="ExternalInput").ap()
    identb = nc.dram_tensor("identb", [128, 128], BF16, kind="ExternalInput").ap()
    identh = nc.dram_tensor("identh", [128, 128], F16, kind="ExternalInput").ap()
    qscl = nc.dram_tensor("qscl", [128, 1], F32, kind="ExternalInput").ap()

    att_h = nc.dram_tensor("att_h", [DM, C], BF16, kind="Internal").ap()
    s_h = nc.dram_tensor("s_h", [C, DM], BF16, kind="Internal").ap()
    y_h = nc.dram_tensor("y_h", [C, DM], F32, kind="Internal").ap()
    out = nc.dram_tensor("out", [C, DM], F32, kind="ExternalOutput").ap()

    wkv_v = wkv.rearrange("(c p) n -> p c n", p=128)
    w2t_v = w2t.rearrange("(c p) n -> p c n", p=128)
    att_hv = att_h.rearrange("(c p) t -> p c t", p=128)

    em = Emit()

    with ExitStack() as top:
        sem = lambda name, step: Cnt(top.enter_context(nc.semaphore(name)), step)
        qconst = sem("qconst", 16)
        qmisc = sem("qmisc", 16)
        qx = [sem(f"qx{i}", 16) for i in range(2)]
        qwq = [sem(f"qwq{i}", 16) for i in range(2)]
        qwo = [sem(f"qwo{i}", 16) for i in range(2)]
        qatt = [sem(f"qatt{i}", 16) for i in range(2)]
        qs = [sem(f"qs{i}", 16) for i in range(2)]
        qscs = [sem(f"qscs{i}", 16) for i in range(2)]
        qw13 = [sem(f"qw13{i}", 16) for i in range(2)]
        qw2 = [sem(f"qw2{i}", 16) for i in range(2)]
        qy = [sem(f"qy{i}", 16) for i in range(2)]
        qc = [sem(f"qc{i}", 16) for i in range(2)]
        qout = [sem(f"qout{i}", 16) for i in range(2)]
        allq = [qconst, qmisc] + qx + qwq + qwo + qatt + qs + qscs + qw13 + \
            qw2 + qy + qc + qout
        spe = sem("spe", 1)
        sve = sem("sve", 1)
        sse = sem("sse", 1)
        sge = sem("sge", 1)

        SB = lambda st, name, shape, dt: st.enter_context(nc.sbuf_tensor(name, shape, dt))
        PS = lambda st, name, shape, dt: st.enter_context(nc.psum_tensor(name, shape, dt))

        idb_sb = SB(top, "idb_sb", [128, 128], BF16)
        qscl_sb = SB(top, "qscl_sb", [128, 1], F32)
        idh_sb = SB(top, "idh_sb", [128, 128], F16)
        xnst = ExitStack()
        xnT = SB(xnst, "xnT", [128, KC * 1024], BF16)
        ph01 = ExitStack()
        K_sb = SB(ph01, "K_sb", [128, S], BF16)
        V_sb = SB(ph01, "V_sb", [128, KC * 128], BF16)
        dlt_sb = SB(ph01, "dlt_sb", [128, 4096], F16)

        ph0 = ExitStack()
        xbuf = [SB(ph0, f"xbuf{m}", [128, DM], F32) for m in range(2)]
        sqbuf = SB(ph0, "sqbuf", [128, DM], F32)
        xn_sb = [SB(ph0, f"xn{m}", [128, DM], BF16) for m in range(2)]
        xnt_tmp = [SB(ph0, f"xnt_tmp{m}", [128, KC * 128], BF16) for m in range(2)]
        wkv_sb = SB(ph0, "wkv_sb", [128, KC * 256], BF16)
        stats = {}
        for nm in ("sumx", "sumsq", "negmean", "var", "vareps", "std", "rstd",
                   "negmr"):
            stats[nm] = [SB(ph0, f"{nm}{m}", [128, 1], F32) for m in range(2)]
        tr_ps = [PS(ph0, f"tr_ps{m}", [128, 1024], BF16) for m in range(2)]
        k_ps = [PS(ph0, f"k_ps{m}", [128, 512], F32) for m in range(2)]
        v_ps = [PS(ph0, f"v_ps{m}", [128, 512], F32) for m in range(2)]

        em.op("gpsimd", lambda g: g.dma_start(
            wkv_sb[:].rearrange("p (c n) -> p c n", c=KC), wkv_v),
            inc=qconst)
        em.op("gpsimd", lambda g: g.dma_start(dlt_sb[:], dlt), inc=qconst)
        em.op("gpsimd", lambda g: g.dma_start(idb_sb[:], identb), inc=qconst)
        em.op("gpsimd", lambda g: g.dma_start(idh_sb[:], identh), inc=qconst)
        em.op("gpsimd", lambda g: g.dma_start(qscl_sb[:], qscl), inc=qconst)
        q_const = (qconst, qconst.n)

        mark = {}

        def phase0_tile(i, src_rows, own):
            m = i % 2
            waits = []
            if mark.get(("xapply", i - 2)) is not None:
                waits.append((sse, mark[("xapply", i - 2)]))
            em.op("sync", lambda sy, sr=src_rows: sy.dma_start(xbuf[m][:], sr),
                  waits=waits, inc=qx[m])
            q_x = (qx[m], qx[m].n)
            waits = [q_x]
            if ("stats", i - 2) in mark:
                waits.append((sve, mark[("stats", i - 2)]))
            em.op("scalar", lambda s: s.activation(
                sqbuf[:], xbuf[m][:], AF.Square, accum_out=stats["sumsq"][m][:]),
                waits=waits, inc=sse)
            a0 = sse.n
            em.op("vector", lambda v: v.tensor_reduce(
                stats["sumx"][m][:], xbuf[m][:], AX.X, AluOpType.add),
                waits=waits, inc=sve)
            g1 = sve.n
            em.op("vector", lambda v: v.tensor_scalar(
                stats["negmean"][m][:], stats["sumx"][m][:], -1.0 / DM, None,
                AluOpType.mult), waits=[(sve, g1)], inc=sve)
            g2 = sve.n
            em.op("vector", lambda v: v.scalar_tensor_tensor(
                stats["var"][m][:], stats["sumx"][m][:], stats["negmean"][m][:],
                stats["sumsq"][m][:], AluOpType.mult, AluOpType.add),
                waits=[(sve, g2), (sse, a0)], inc=sve)
            g3 = sve.n
            em.op("vector", lambda v: v.tensor_scalar(
                stats["vareps"][m][:], stats["var"][m][:], 1.0 / DM, EPS,
                AluOpType.mult, AluOpType.add), waits=[(sve, g3)], inc=sve)
            g3b = sve.n
            em.op("scalar", lambda s: s.activation(
                stats["std"][m][:], stats["vareps"][m][:], AF.Sqrt),
                waits=[(sve, g3b)], inc=sse)
            a1 = sse.n
            em.op("vector", lambda v: v.reciprocal(
                stats["rstd"][m][:], stats["std"][m][:]), waits=[(sse, a1)], inc=sve)
            g4 = sve.n
            em.op("vector", lambda v: v.tensor_scalar(
                stats["negmr"][m][:], stats["sumx"][m][:], stats["rstd"][m][:],
                -1.0 / DM, AluOpType.mult, AluOpType.mult),
                waits=[(sve, g4)], inc=sve)
            g5 = sve.n
            mark[("stats", i)] = g5
            waits = [(sve, g5)]
            if ("tr", i - 2, 3) in mark:
                waits.append((spe, mark[("tr", i - 2, 3)]))
            em.op("scalar", lambda s: s.activation(
                xn_sb[m][:], xbuf[m][:], AF.Identity,
                bias=stats["negmr"][m][:], scale=stats["rstd"][m][:]),
                waits=waits, inc=sse)
            a2 = sse.n
            mark[("xapply", i)] = a2
            for r in range(4):
                mb2 = r % 2
                waits = [(sse, a2), q_const]
                if ("trevac", i, r - 2) in mark:
                    waits.append(mark[("trevac", i, r - 2)])
                elif ("trevac", i - 1, r + 2) in mark:
                    waits.append(mark[("trevac", i - 1, r + 2)])
                for cc in range(8):
                    c = r * 8 + cc
                    em.op("tensor", lambda t, c=c, cc=cc, mb2=mb2, m=m: t.matmul(
                        tr_ps[mb2][:, cc * 128:(cc + 1) * 128],
                        xn_sb[m][:, c * 128:(c + 1) * 128], idb_sb[:],
                        start=(cc == 0), stop=(cc == 7), is_transpose=True),
                        waits=(waits if cc == 0 else ()),
                        inc=(spe if cc == 7 else None))
                mark[("tr", i, r)] = spe.n
                if own:
                    dst = xnT[:].rearrange("p (c t) -> p c t", t=1024)[
                        :, r * 8:(r + 1) * 8, (i % 8) * 128:((i % 8) + 1) * 128]
                else:
                    dst = xnt_tmp[m][:, r * 1024:(r + 1) * 1024].rearrange(
                        "p (c t) -> p c t", t=128)
                em.op("vector", lambda v, dst=dst, mb2=mb2: v.tensor_copy(
                    dst, tr_ps[mb2][:].rearrange("p (c t) -> p c t", t=128)),
                    waits=[(spe, mark[("tr", i, r)])], inc=sve)
                mark[("trevac", i, r)] = (sve, sve.n)
            if own:
                return
            waits = [mark[("trevac", i, 3)], q_const]
            if ("kve", i - 2) in mark:
                waits.append((sve, mark[("kve", i - 2)]))
            for c in range(KC):
                src_c = xnt_tmp[m][:, c * 128:(c + 1) * 128]
                em.op("tensor", lambda t, c=c, src_c=src_c: t.matmul(
                    k_ps[m][:, 0:128], wkv_sb[:, c * 256:c * 256 + 128], src_c,
                    start=(c == 0), stop=(c == KC - 1)),
                    waits=(waits if c == 0 else ()),
                    inc=(spe if c == KC - 1 else None))
            for c in range(KC):
                src_c = xnt_tmp[m][:, c * 128:(c + 1) * 128]
                em.op("tensor", lambda t, c=c, src_c=src_c: t.matmul(
                    v_ps[m][:, 0:128], src_c, wkv_sb[:, c * 256 + 128:c * 256 + 256],
                    start=(c == 0), stop=(c == KC - 1)),
                    inc=(spe if c == KC - 1 else None))
            mark[("kv", i)] = spe.n
            em.op("vector", lambda v: v.tensor_copy(
                K_sb[:, i * 128:(i + 1) * 128], k_ps[m][:, 0:128]),
                waits=[(spe, mark[("kv", i)])])
            em.op("vector", lambda v: v.tensor_copy(
                V_sb[:, i * 128:(i + 1) * 128], v_ps[m][:, 0:128]), inc=sve)
            mark[("kve", i)] = sve.n

        for i in range(NT):
            phase0_tile(i, xb[i * 128:(i + 1) * 128, :], own=False)
        for i in range(NT, NT + OT):
            phase0_tile(i, xchunk[(i - NT) * 128:(i - NT + 1) * 128, :], own=True)

        ph0_done_se = sse.n
        ph0_done_ve = sve.n
        ph0_done_pe = spe.n
        ph0.close()

        ph1 = ExitStack()
        wq_sb = [SB(ph1, f"wq{m}", [128, KC * 128], BF16) for m in range(2)]
        qT_sb = [SB(ph1, f"qT{m}", [128, C], BF16) for m in range(2)]
        t_sb = [SB(ph1, f"t{m}", [128, S], F16) for m in range(2)]
        p_sb = [SB(ph1, f"pp{m}", [128, S], F16) for m in range(2)]
        ptA = SB(ph1, "ptA", [128, 16 * 512], BF16)
        ptB = SB(ph1, "ptB", [128, 32 * 512], BF16)
        negmx = [SB(ph1, f"negmx{m}", [128, 1], F32) for m in range(2)]
        mx8 = [SB(ph1, f"mx8{m}", [128, 8], F32) for m in range(2)]
        den = [SB(ph1, f"den{m}", [128, 1], F32) for m in range(2)]
        rec = [SB(ph1, f"rec{m}", [128, 1], F32) for m in range(2)]
        d_sb = [SB(ph1, f"dd{m}", [128, 128], F16) for m in range(2)]
        att_st = [SB(ph1, f"attst{m}", [128, C], BF16) for m in range(2)]
        q_ps = [PS(ph1, f"q_ps{m}", [128, 512], F32) for m in range(2)]
        sc_ps = [PS(ph1, f"sc_ps{m}", [128, 512], F32) for m in range(2)]
        pt_ps = [PS(ph1, f"pt_ps{m}", [128, 512], F32) for m in range(2)]
        av_ps = [PS(ph1, f"av_ps{m}", [128, 512], F32) for m in range(2)]

        em.op("vector", lambda v: v.memset(ptA[:], 0.0),
              waits=[(sse, ph0_done_se), (sve, ph0_done_ve), (spe, ph0_done_pe)])
        em.op("vector", lambda v: v.memset(ptB[:], 0.0), inc=sve)
        pt_zeroed = sve.n

        def emit_head(h):
            m = h % 2
            if h < 2:
                waits = [(sse, ph0_done_se), (sve, ph0_done_ve),
                         (spe, ph0_done_pe)]
            else:
                waits = [(spe, mark[("qp", h - 2)])]
            em.op("gpsimd", lambda g: g.dma_start(
                wq_sb[m][:], wq_l[h]), waits=waits, inc=qwq[m])
            mark[("wq", h)] = (qwq[m], qwq[m].n)

        def emit_qp_chunk(h, c0, c1):
            m = h % 2
            waits = []
            if c0 == 0:
                waits = [mark[("wq", h)], (sse, ph0_done_se),
                         (sve, ph0_done_ve)]
                if ("qe", h - 1) in mark:
                    waits.append(mark[("qe", h - 1)])
            first = True
            for c in range(c0, c1):
                for half in range(2):
                    em.op("tensor", lambda t, half=half, c=c, m=m: t.matmul(
                        q_ps[half][:],
                        wq_sb[m][:, c * 128:(c + 1) * 128],
                        xnT[:, c * 1024 + half * 512: c * 1024 + half * 512 + 512],
                        start=(c == 0), stop=(c == KC - 1)),
                        waits=(waits if first else ()),
                        inc=(spe if (c == c1 - 1 and half == 1) else None))
                    first = False
            if c1 == KC:
                mark[("qp", h)] = spe.n

        def emit_qe(h):
            m = h % 2
            for half in range(2):
                em.op("scalar", lambda s, half=half, m=m: s.activation(
                    qT_sb[m][:, half * 512:(half + 1) * 512], q_ps[half][:],
                    AF.Identity, scale=qscl_sb[:]),
                    waits=([(spe, mark[("qp", h)]), q_const]
                           if half == 0 else ()),
                    inc=(sse if half == 1 else None))
            mark[("qe", h)] = (sse, sse.n)

        def emit_head_body(h):
            m = h % 2
            slope = SLOPES[h]
            if h == 0:
                emit_qe(0)
            def emit_sc(qt):
                for kb in range(qt + 1):
                    emit_sc_kb(qt, kb)
                emit_mx(qt)

            def emit_sc_kb(qt, kb):
                mq = qt % 2
                if True:
                    bank = len(seq["stt"]) % 2
                    waits = [mark[("qe", h)], (sve, pt_zeroed)]
                    if len(seq["stt"]) >= 2:
                        waits.append((sve, seq["stt"][-2]))
                    em.op("tensor", lambda t, qt=qt, kb=kb, bank=bank: t.matmul(
                        sc_ps[bank][:],
                        qT_sb[m][:, qt * 128:(qt + 1) * 128],
                        K_sb[:, kb * 512:(kb + 1) * 512],
                        start=True, stop=True),
                        waits=waits, inc=spe)
                    mark[("sc", h, qt, kb)] = spe.n
                    off = (kb - qt) * 512 + 3584
                    waits = [(spe, mark[("sc", h, qt, kb)])]
                    if kb == 0:
                        pe_exp = mark.get(("exp", h, qt - 2)) or mark.get(
                            ("exp", h - 1, qt + QTN - 2))
                        if pe_exp:
                            waits.append((sse, pe_exp))
                    em.op("vector", lambda v, qt=qt, kb=kb, off=off, mq=mq,
                          bank=bank: v.scalar_tensor_tensor(
                              t_sb[mq][:, kb * 512:(kb + 1) * 512],
                              dlt_sb[:, off:off + 512], slope, sc_ps[bank][:],
                              AluOpType.mult, AluOpType.add),
                          waits=waits, inc=sve)
                    seq["stt"].append(sve.n)
                    em.op("vector", lambda v, qt=qt, kb=kb, mq=mq: v.tensor_reduce(
                        mx8[mq][:, kb:kb + 1],
                        t_sb[mq][:, kb * 512:(kb + 1) * 512],
                        AX.X, AluOpType.max),
                        waits=[(sve, seq["stt"][-1])], inc=sve)
                    mark[("mxp", h, qt, kb)] = sve.n

            def emit_mx(qt):
                mq = qt % 2
                em.op("vector", lambda v, mq=mq, qt=qt: v.tensor_reduce(
                    negmx[mq][:], mx8[mq][:, 0:qt + 1], AX.X,
                    AluOpType.max, negate=True),
                    waits=[(sve, mark[("mxp", h, qt, qt)])], inc=sve)
                mark[("mx", h, qt)] = sve.n
                waits = [(sve, mark[("mx", h, qt)])]
                pe_pt = mark.get(("ptdone", h, qt - 2)) or mark.get(
                    ("ptdone", h - 1, qt + QTN - 2))
                if pe_pt:
                    waits.append((spe, pe_pt))
                em.op("scalar", lambda s, mq=mq, qt=qt: s.activation(
                    p_sb[mq][:, 0:(qt + 1) * 512], t_sb[mq][:, 0:(qt + 1) * 512],
                    AF.Exp, bias=negmx[mq][:], accum_out=den[mq][:]),
                    waits=waits, inc=sse)
                mark[("exp", h, qt)] = sse.n

            def emit_rd(qt):
                mq = qt % 2
                em.op("vector", lambda v, mq=mq: v.reciprocal(
                    rec[mq][:], den[mq][:]),
                    waits=[(sse, mark[("exp", h, qt)])], inc=sve)
                rr = sve.n
                waits = [(sve, rr)]
                pe_pt = mark.get(("ptdone", h, qt - 2)) or mark.get(
                    ("ptdone", h - 1, qt + QTN - 2))
                if pe_pt:
                    waits.append((spe, pe_pt))
                em.op("vector", lambda v, mq=mq: v.tensor_scalar(
                    d_sb[mq][:], idh_sb[:], rec[mq][:], None, AluOpType.mult),
                    waits=waits, inc=sve)
                mark[("dd", h, qt)] = sve.n

            def emit_pt_one(qt, g):
                mq = qt % 2
                qg = qt // 4
                dstbuf = ptA if qg == 0 else ptB
                if True:
                    bank = len(seq["pte"]) % 2
                    waits = [(sse, mark[("exp", h, qt)]), (sve, mark[("dd", h, qt)])]
                    if len(seq["pte"]) >= 2:
                        waits.append((sse, seq["pte"][-2]))
                    for kk in range(4):
                        ks = g * 4 + kk
                        em.op("tensor", lambda t, ks=ks, kk=kk, mq=mq,
                              bank=bank: t.matmul(
                            pt_ps[bank][:, kk * 128:(kk + 1) * 128],
                            p_sb[mq][:, ks * 128:(ks + 1) * 128], d_sb[mq][:],
                            start=(kk == 0), stop=(kk == 3)),
                            waits=(waits if kk == 0 else ()),
                            inc=(spe if kk == 3 else None))
                    mark[("ptg", h, qt, g)] = spe.n
                    dst = dstbuf[:].rearrange("p (c q) -> p c q", q=512)[
                        :, g * 4:(g + 1) * 4, (qt % 4) * 128:((qt % 4) + 1) * 128]
                    waits = [(spe, mark[("ptg", h, qt, g)])]
                    if qt % 4 == 0 and g == 0:
                        prev_av = mark.get(("av", h, qt // 4 - 1)) or mark.get(
                            ("av", h - 1, qt // 4))
                        if prev_av:
                            waits.append((spe, prev_av))
                    em.op("scalar", lambda s, dst=dst, bank=bank: s.copy(
                        dst, pt_ps[bank][:].rearrange("p (c q) -> p c q", q=128)),
                        waits=waits, inc=sse)
                    seq["pte"].append(sse.n)
                if g == qt:
                    mark[("ptdone", h, qt)] = spe.n
                    mark[("ptedone", h, qt)] = sse.n

            def emit_av(qg):
                srcbuf = ptA if qg == 0 else ptB
                ncks = 16 * qg + 16
                waits = [(sse, mark[("ptedone", h, qg * 4 + 3)])]
                prev = mark.get(("ave", h, qg - 2)) or mark.get(("ave", h - 1, qg))
                if prev:
                    waits.append((sse, prev))
                for ck in range(ncks):
                    em.op("tensor", lambda t, ck=ck, qg=qg, srcbuf=srcbuf: t.matmul(
                        av_ps[qg % 2][:],
                        V_sb[:, ck * 128:(ck + 1) * 128],
                        srcbuf[:, ck * 512:(ck + 1) * 512],
                        start=(ck == 0), stop=(ck == ncks - 1)),
                        waits=(waits if ck == 0 else ()),
                        inc=(spe if ck == ncks - 1 else None))
                mark[("av", h, qg)] = spe.n
                waits = [(spe, mark[("av", h, qg)])]
                if ("att_store", h - 2) in mark:
                    waits.append(mark[("att_store", h - 2)])
                em.op("scalar", lambda s, qg=qg: s.copy(
                    att_st[m][:, qg * 512:(qg + 1) * 512], av_ps[qg % 2][:]),
                    waits=waits, inc=sse)
                mark[("ave", h, qg)] = sse.n

            emit_sc(0)
            for qt in range(1, QTN):
                ptg_next = 0
                for kb in range(qt + 1):
                    emit_sc_kb(qt, kb)
                    if kb == min(1, qt):
                        emit_rd(qt - 1)
                    if kb >= 3 and ptg_next < qt:
                        emit_pt_one(qt - 1, ptg_next)
                        ptg_next += 1
                while ptg_next < qt:
                    emit_pt_one(qt - 1, ptg_next)
                    ptg_next += 1
                emit_mx(qt)
                if qt - 1 == 3:
                    emit_av(0)
                if qt >= 4 and h + 1 < NH:
                    emit_qp_chunk(h + 1, (qt - 4) * 8, (qt - 3) * 8)
                    if qt == QTN - 1:
                        emit_qe(h + 1)
            emit_rd(QTN - 1)
            for g in range(QTN):
                emit_pt_one(QTN - 1, g)
            emit_av(1)
            em.op("sync", lambda sy: sy.dma_start(
                att_h[h * 128:(h + 1) * 128, :], att_st[m][:]),
                waits=[(sse, mark[("ave", h, 1)])], inc=qatt[m])
            mark[("att_store", h)] = (qatt[m], qatt[m].n)

        seq = {"stt": [], "pte": []}
        emit_head(0)
        emit_head(1)
        emit_qp_chunk(0, 0, KC)
        for h in range(NH):
            if h + 2 < NH:
                emit_head(h + 2)
            emit_head_body(h)

        ph1_stores = [(qatt[0], qatt[0].n), (qatt[1], qatt[1].n)]
        ph1.close()
        ph01.close()

        ph2 = ExitStack()
        att_all = SB(ph2, "att_all", [128, KC * 1024], BF16)
        wo_sb = [SB(ph2, f"wo{m}", [128, KC * 512], BF16) for m in range(2)]
        s_st = [SB(ph2, f"s_st{m}", [128, 512], BF16) for m in range(2)]
        s_ps = [PS(ph2, f"s_ps{m}", [128, 512], F32) for m in range(2)]

        em.op("sync", lambda sy: sy.dma_start(
            att_all[:].rearrange("p (c t) -> p c t", t=1024), att_hv),
            waits=ph1_stores, inc=qmisc)
        q_attall = (qmisc, qmisc.n)

        def emit_wo_load(dt):
            mw = dt % 2
            if dt < 2:
                waits = list(ph1_stores)
            else:
                waits = [(spe, mark[("wopd", dt - 2)])]
            em.op("gpsimd", lambda g, dt=dt, mw=mw: g.dma_start(
                wo_sb[mw][:], wo_l[dt]), waits=waits, inc=qwo[mw])
            mark[("wol", dt)] = (qwo[mw], qwo[mw].n)

        emit_wo_load(0)
        emit_wo_load(1)
        for dt in range(8):
            mw = dt % 2
            for qt in range(QTN):
                waits = [mark[("wol", dt)], q_attall]
                prev = mark.get(("se2", dt, qt - 2)) or mark.get(("se2", dt - 1, qt + 6))
                if prev:
                    waits.append((sse, prev))
                for hc in range(KC):
                    em.op("tensor", lambda t, dt=dt, qt=qt, hc=hc, mw=mw: t.matmul(
                        s_ps[qt % 2][:],
                        att_all[:, hc * 1024 + qt * 128: hc * 1024 + qt * 128 + 128],
                        wo_sb[mw][:, hc * 512:(hc + 1) * 512],
                        start=(hc == 0), stop=(hc == KC - 1)),
                        waits=(waits if hc == 0 else ()),
                        inc=(spe if hc == KC - 1 else None))
                mark[("wopq", dt, qt)] = spe.n
                waits = [(spe, mark[("wopq", dt, qt)])]
                prev = mark.get(("s_store", dt, qt - 2)) or mark.get(
                    ("s_store", dt - 1, qt + 6))
                if prev:
                    waits.append(prev)
                em.op("scalar", lambda s, qt=qt: s.copy(
                    s_st[qt % 2][:], s_ps[qt % 2][:]), waits=waits, inc=sse)
                mark[("se2", dt, qt)] = sse.n
                em.op("sync", lambda sy, dt=dt, qt=qt: sy.dma_start(
                    s_h[qt * 128:(qt + 1) * 128, dt * 512:(dt + 1) * 512],
                    s_st[qt % 2][:]),
                    waits=[(sse, mark[("se2", dt, qt)])], inc=qs[qt % 2])
                mark[("s_store", dt, qt)] = (qs[qt % 2], qs[qt % 2].n)
            mark[("wopd", dt)] = spe.n
            if dt + 2 < 8:
                emit_wo_load(dt + 2)

        s_stores = [(qs[0], qs[0].n), (qs[1], qs[1].n)]
        ph2.close()

        ph3 = ExitStack()
        scs_sb = [SB(ph3, f"scs{m}", [128, 3 * KC], F32) for m in range(2)]
        w1r = [SB(ph3, f"w1r{m}", [128, KC * 128], BF16) for m in range(2)]
        w3r = [SB(ph3, f"w3r{m}", [128, KC * 128], BF16) for m in range(2)]
        w2g = SB(ph3, "w2g", [128, G2 * DM], BF16)
        hT = SB(ph3, "hT", [128, G2 * 1024], BF16)
        gs_tmp = SB(ph3, "gs_tmp", [128, 1024], F32)
        y_st = [SB(ph3, f"y_st{m}", [128, 2048], F32) for m in range(2)]
        g_ps = PS(ph3, "g_ps", [128, 1024], F32)
        u_ps = PS(ph3, "u_ps", [128, 1024], F32)
        y_ps = [PS(ph3, f"y_ps{m}", [128, 512], F32) for m in range(2)]
        y_ps2 = [PS(ph3, f"y_ps2{m}", [128, 512], F32) for m in range(2)]

        def dq_fn(v, buf, scoff, msc, nb):
            t3 = buf.rearrange("p (c n) -> p c n", n=128)
            scb = scs_sb[msc][:, scoff * KC:(scoff + 1) * KC].unsqueeze(
                2).broadcast_to([128, KC, 128])
            return v.tensor_tensor(t3, t3, scb, AluOpType.mult)

        def emit_w13(nb):
            m = nb % 2
            waits = []
            if ("f1h", nb - 2, 1) in mark:
                waits.append((spe, mark[("f1h", nb - 2, 1)]))
            else:
                waits.extend(s_stores)
            swaits = list(waits)
            if ("dq2", nb - 2) in mark:
                swaits.append((sve, mark[("dq2", nb - 2)]))
            em.op("sync", lambda sy, nb=nb, m=m: sy.dma_start(
                scs_sb[m][:, 0:KC], sc1[:, nb * KC:(nb + 1) * KC]),
                waits=swaits, inc=qscs[m])
            em.op("sync", lambda sy, nb=nb, m=m: sy.dma_start(
                scs_sb[m][:, KC:2 * KC], sc3[:, nb * KC:(nb + 1) * KC]),
                inc=qscs[m])
            em.op("sync", lambda sy, nb=nb, m=m: sy.dma_start(
                scs_sb[m][:, 2 * KC:3 * KC], sc2[:, nb * KC:(nb + 1) * KC]),
                inc=qscs[m])
            mark[("scl", nb)] = (qscs[m], qscs[m].n)
            em.op("gpsimd", lambda g, nb=nb, m=m: g.dma_start(
                w1r[m][:], w1l[nb]), waits=waits, inc=qw13[m])
            em.op("gpsimd", lambda g, nb=nb, m=m: g.dma_start(
                w3r[m][:], w3l[nb]), inc=qw13[m])
            q_w = (qw13[m], qw13[m].n)
            em.op("vector", lambda v, m=m, nb=nb: dq_fn(v, w1r[m][:], 0, m, nb),
                  waits=[q_w, mark[("scl", nb)]])
            em.op("vector", lambda v, m=m, nb=nb: dq_fn(v, w3r[m][:], 1, m, nb),
                  inc=sve)
            mark[("dq13", nb)] = sve.n

        def emit_w2(nb):
            gidx = nb // G2
            slot = nb % G2
            m = nb % 2
            waits = []
            if ("f2", gidx - 1) in mark:
                waits.append((spe, mark[("f2", gidx - 1)]))
            elif gidx == 0:
                waits.extend(s_stores)
            if ("dq2", nb - 2) in mark:
                waits.append((sve, mark[("dq2", nb - 2)]))
            em.op("gpsimd", lambda g, nb=nb, slot=slot: g.dma_start(
                w2g[:, slot * DM:(slot + 1) * DM], w2t_v[:, nb, :]),
                waits=waits, inc=qw2[m])
            q_w = (qw2[m], qw2[m].n)
            em.op("vector", lambda v, slot=slot, nb=nb, m=m: dq_fn(
                v, w2g[:, slot * DM:(slot + 1) * DM], 2, m, nb),
                waits=[q_w, mark[("scl", nb)]], inc=sve)
            mark[("dq2", nb)] = sve.n

        emit_w13(0)
        emit_w2(0)
        for nb in range(NB):
            m = nb % 2
            gidx = nb // G2
            slot = nb % G2
            if nb + 1 < NB:
                emit_w13(nb + 1)
            for half in range(2):
                waits = [(sve, mark[("dq13", nb)]), (sse, ph0_done_se),
                         (sve, ph0_done_ve)]
                if ("hm", nb - 1, half) in mark:
                    waits.append((sve, mark[("hm", nb - 1, half)]))
                if slot == 0 and half == 0 and ("ye", gidx - 1, 7, 7) in mark:
                    waits.append((sse, mark[("ye", gidx - 1, 7, 7)]))
                first = True
                for c in range(KC):
                    em.op("tensor", lambda t, c=c, half=half, m=m: t.matmul(
                        g_ps[:, half * 512:(half + 1) * 512],
                        w1r[m][:, c * 128:(c + 1) * 128],
                        xnT[:, c * 1024 + half * 512: c * 1024 + half * 512 + 512],
                        start=(c == 0), stop=(c == KC - 1)),
                        waits=(waits if first else ())); first = False
                    em.op("tensor", lambda t, c=c, half=half, m=m: t.matmul(
                        u_ps[:, half * 512:(half + 1) * 512],
                        w3r[m][:, c * 128:(c + 1) * 128],
                        xnT[:, c * 1024 + half * 512: c * 1024 + half * 512 + 512],
                        start=(c == 0), stop=(c == KC - 1)),
                        inc=(spe if c == KC - 1 else None))
                mark[("f1h", nb, half)] = spe.n
                em.op("scalar", lambda s, half=half: s.activation(
                    gs_tmp[:, half * 512:(half + 1) * 512],
                    g_ps[:, half * 512:(half + 1) * 512], AF.Silu),
                    waits=[(spe, mark[("f1h", nb, half)])], inc=sse)
                mark[("si", nb, half)] = sse.n
                em.op("vector", lambda v, slot=slot, half=half: v.tensor_tensor(
                    hT[:, slot * 1024 + half * 512: slot * 1024 + half * 512 + 512],
                    gs_tmp[:, half * 512:(half + 1) * 512],
                    u_ps[:, half * 512:(half + 1) * 512],
                    AluOpType.mult), waits=[(sse, mark[("si", nb, half)])],
                    inc=sve)
                mark[("hm", nb, half)] = sve.n
            if slot == G2 - 1:
                g = gidx
                ybank = [g_ps[:, 0:512], g_ps[:, 512:1024],
                         u_ps[:, 0:512], u_ps[:, 512:1024],
                         y_ps[0][:], y_ps[1][:], y_ps2[0][:], y_ps2[1][:]]
                for tb in range(8):
                    for nn in range(G2):
                        for dt in range(8):
                            waits = []
                            if nn == 0:
                                waits = [(sve, mark[("hm", nb, 1)]),
                                         (sve, mark[("dq2", nb)])]
                                ye_prev = mark.get(("ye", g, tb - 1, dt)) or \
                                    mark.get(("ye", g - 1, 7, dt))
                                if ye_prev:
                                    waits.append((sse, ye_prev))
                            em.op("tensor", lambda t, tb=tb, dt=dt, nn=nn: t.matmul(
                                ybank[dt],
                                hT[:, nn * 1024 + tb * 128: nn * 1024 + tb * 128 + 128],
                                w2g[:, nn * DM + dt * 512: nn * DM + (dt + 1) * 512],
                                start=(nn == 0), stop=(nn == G2 - 1)),
                                waits=waits,
                                inc=(spe if (nn == G2 - 1 and dt == 7) else None))
                    mark[("f2t", g, tb)] = spe.n
                    for dt in range(8):
                        half = dt // 4
                        ybuf = (tb * 2 + half) % 2
                        waits = [(spe, mark[("f2t", g, tb)])]
                        if dt % 4 == 0:
                            prev = mark.get(("y_store", g, tb - 1, half)) or \
                                mark.get(("y_store", g - 1, 7, half))
                            if prev:
                                waits.append(prev)
                        em.op("scalar", lambda s, dt=dt, ybuf=ybuf: s.copy(
                            y_st[ybuf][:, (dt % 4) * 512:((dt % 4) + 1) * 512],
                            ybank[dt]), waits=waits, inc=sse)
                        mark[("ye", g, tb, dt)] = sse.n
                        if dt % 4 == 3:
                            em.op("gpsimd", lambda gp, tb=tb, g=g, half=half,
                                  ybuf=ybuf: gp.dma_start(
                                      y_h[tb * 128:(tb + 1) * 128,
                                          half * 2048:(half + 1) * 2048],
                                      y_st[ybuf][:],
                                      **({"accum_op": AluOpType.add}
                                         if g > 0 else {})),
                                  waits=[(sse, mark[("ye", g, tb, dt)])],
                                  inc=qy[ybuf])
                            mark[("y_store", g, tb, half)] = (qy[ybuf],
                                                              qy[ybuf].n)
                mark[("f2", g)] = spe.n
            if nb + 1 < NB:
                emit_w2(nb + 1)

        y_stores = [(qy[0], qy[0].n), (qy[1], qy[1].n)]
        ph3.close()
        xnst.close()

        ph4 = ExitStack()
        y_in = [SB(ph4, f"y_in{m}", [128, DM], F32) for m in range(2)]
        si_in = [SB(ph4, f"si_in{m}", [128, DM], BF16) for m in range(2)]
        x_in = [SB(ph4, f"x_in{m}", [128, DM], F32) for m in range(2)]
        gf_tmp = [SB(ph4, f"gf{m}", [128, DM], F32) for m in range(2)]
        sg_tmp = [SB(ph4, f"sg{m}", [128, DM], F32) for m in range(2)]
        out_t = [SB(ph4, f"out_t{m}", [128, DM], F32) for m in range(2)]

        for tb in range(OT):
            m = tb % 2
            waits = list(y_stores) + list(s_stores)
            if ("cmb", tb - 2) in mark:
                waits.append((sve, mark[("cmb", tb - 2)]))
            em.op("sync", lambda sy, tb=tb, m=m: sy.dma_start(
                y_in[m][:], y_h[tb * 128:(tb + 1) * 128, :]), waits=waits,
                inc=qc[m])
            em.op("sync", lambda sy, tb=tb, m=m: sy.dma_start(
                si_in[m][:], s_h[tb * 128:(tb + 1) * 128, :]), inc=qc[m])
            em.op("sync", lambda sy, tb=tb, m=m: sy.dma_start(
                x_in[m][:], xchunk[tb * 128:(tb + 1) * 128, :]), inc=qc[m])
            q_c = (qc[m], qc[m].n)
            waits = [q_c]
            if ("cmb_o", tb - 2) in mark:
                waits.append(mark[("cmb_o", tb - 2)])
            em.op("scalar", lambda s, m=m: s.activation(
                gf_tmp[m][:], y_in[m][:], AF.Silu), waits=waits, inc=sse)
            c1 = sse.n
            em.op("vector", lambda v, m=m: v.tensor_tensor(
                sg_tmp[m][:], si_in[m][:], gf_tmp[m][:], AluOpType.mult),
                waits=[(sse, c1)], inc=sve)
            c2 = sve.n
            em.op("vector", lambda v, m=m: v.tensor_tensor(
                out_t[m][:], sg_tmp[m][:], x_in[m][:], AluOpType.add),
                waits=[(sve, c2)], inc=sve)
            mark[("cmb", tb)] = sve.n
            em.op("sync", lambda sy, tb=tb, m=m: sy.dma_start(
                out[tb * 128:(tb + 1) * 128, :], out_t[m][:]),
                waits=[(sve, mark[("cmb", tb)])], inc=qout[m])
            mark[("cmb_o", tb)] = (qout[m], qout[m].n)
        ph4.close()

        totals = [(q.h, q.n) for q in allq if q.n > 0]

        def final_wait(eng):
            for hh, vv in totals:
                eng.wait_ge(hh, vv)
        em.streams["sync"].append(final_wait)

        with nc.Block() as blk:
            @blk.tensor
            def _(eng):
                for fn in em.streams["tensor"]:
                    fn(eng)

            @blk.vector
            def _(eng):
                for fn in em.streams["vector"]:
                    fn(eng)

            @blk.scalar
            def _(eng):
                for fn in em.streams["scalar"]:
                    fn(eng)

            @blk.gpsimd
            def _(eng):
                for fn in em.streams["gpsimd"]:
                    fn(eng)

            @blk.sync
            def _(eng):
                for fn in em.streams["sync"]:
                    fn(eng)

    return nc


def _host_inputs(L0_attn_hidden_states, L0_attn_Wqkv, L0_attn_Wo,
                 L0_ffn_w1, L0_ffn_w1_scale, L0_ffn_w3, L0_ffn_w3_scale,
                 L0_ffn_w2, L0_ffn_w2_scale):
    x = np.ascontiguousarray(np.asarray(L0_attn_hidden_states, dtype=np.float32))
    wqkv = np.asarray(L0_attn_Wqkv, dtype=np.float32)
    wo = np.asarray(L0_attn_Wo, dtype=np.float32)

    def tile_kn(wt, nblocks):
        K, N = wt.shape
        return np.ascontiguousarray(
            wt.reshape(K // 128, 128, nblocks, N // nblocks)
            .transpose(2, 1, 0, 3).reshape(nblocks, 128, -1))

    wq_l = tile_kn(wqkv[:, :4096], NH)
    wkv = np.ascontiguousarray(wqkv[:, 4096:])
    wo_l = tile_kn(wo, 8)
    w1l = tile_kn(np.asarray(L0_ffn_w1, dtype=np.float32).T, NB)
    w3l = tile_kn(np.asarray(L0_ffn_w3, dtype=np.float32).T, NB)
    w2t = np.ascontiguousarray(np.asarray(L0_ffn_w2, dtype=np.float32).T)
    s1 = np.asarray(L0_ffn_w1_scale, dtype=np.float32)
    s3 = np.asarray(L0_ffn_w3_scale, dtype=np.float32)
    s2 = np.asarray(L0_ffn_w2_scale, dtype=np.float32)
    sc1 = np.ascontiguousarray(np.broadcast_to(s1.reshape(1, -1), (128, NB * KC)))
    sc3 = np.ascontiguousarray(np.broadcast_to(s3.reshape(1, -1), (128, NB * KC)))
    sc2 = np.ascontiguousarray(np.broadcast_to(
        np.ascontiguousarray(s2.T).reshape(1, -1), (128, NB * KC)))
    identb = np.eye(128, dtype=ml_dtypes.bfloat16)
    identh = np.eye(128, dtype=np.float16)

    in_maps = []
    for core in range(8):
        b, j = core // 4, core % 4
        rows = np.concatenate([np.arange(512 * k + 128 * j, 512 * k + 128 * j + 128)
                               for k in range(QTN)])
        cc = np.arange(4096, dtype=np.float64)[None, :]
        pq = np.arange(128, dtype=np.float64)[:, None]
        v = (cc - 3584.0 - 128.0 * j) - pq
        tmpl = np.where(v <= 0, v, SENT).astype(np.float16)
        in_maps.append(dict(
            xb=np.ascontiguousarray(x[b]),
            xchunk=np.ascontiguousarray(x[b][rows]),
            wq_l=wq_l, wkv=wkv, wo_l=wo_l, w1l=w1l, w3l=w3l, w2t=w2t,
            sc1=sc1, sc3=sc3, sc2=sc2,
            dlt=tmpl, identb=identb, identh=identh,
            qscl=np.full((128, 1), QSCALE, dtype=np.float32),
        ))
    return in_maps


def kernel(L0_attn_hidden_states, L0_attn_Wqkv, L0_attn_Wo,
           L0_ffn_w1, L0_ffn_w1_scale, L0_ffn_w3, L0_ffn_w3_scale,
           L0_ffn_w2, L0_ffn_w2_scale):
    in_maps = _host_inputs(L0_attn_hidden_states, L0_attn_Wqkv, L0_attn_Wo,
                           L0_ffn_w1, L0_ffn_w1_scale, L0_ffn_w3, L0_ffn_w3_scale,
                           L0_ffn_w2, L0_ffn_w2_scale)
    nc = build_program()
    res = run_bass_kernel_spmd(nc, in_maps, core_ids=list(range(8)))
    out = np.empty((B, S, DM), dtype=np.float32)
    for core in range(8):
        b, j = core // 4, core % 4
        rows = np.concatenate([np.arange(512 * k + 128 * j, 512 * k + 128 * j + 128)
                               for k in range(QTN)])
        out[b][rows] = np.asarray(res.results[core]["out"], dtype=np.float32)
    return out


# version 18
# speedup vs baseline: 1.2635x; 1.0198x over previous
import math
import numpy as np
import ml_dtypes
from contextlib import ExitStack

import concourse.bass as bass
import concourse.mybir as mybir
from concourse.alu_op_type import AluOpType
from concourse.bass_utils import run_bass_kernel_spmd

F32 = mybir.dt.float32
BF16 = mybir.dt.bfloat16
F16 = mybir.dt.float16
AF = mybir.ActivationFunctionType
AX = mybir.AxisListType

B, S, DM, C = 2, 4096, 4096, 1024
NH, DH, I = 32, 128, 14336
KC = DM // 128
NB = I // 128
NT = S // 128
OT = C // 128
QTN = C // 128
KBN = S // 512
G2 = 8
EPS = 1e-5
QSCALE = float(np.float32(DH ** -0.5))
SENT = -30000.0


def _alibi_slopes(n):
    cp2 = 2 ** math.floor(math.log2(n))
    base = 2.0 ** (-(2.0 ** (-(math.log2(cp2) - 3))))
    slopes = base ** np.arange(1, cp2 + 1, dtype=np.float64)
    if cp2 != n:
        extra_base = 2.0 ** (-(2.0 ** (-(math.log2(2 * cp2) - 3))))
        rem = min(cp2, n - cp2)
        extra = extra_base ** np.arange(1, 1 + 2 * rem, 2, dtype=np.float64)
        slopes = np.concatenate([slopes, extra])
    return slopes.astype(np.float32)


SLOPES = [float(s) for s in _alibi_slopes(NH)]


class Cnt:

    def __init__(self, handle, step):
        self.h = handle
        self.step = step
        self.n = 0

    def bump(self):
        self.n += self.step
        return self.n


class Emit:

    def __init__(self):
        self.streams = {k: [] for k in ("tensor", "vector", "scalar", "gpsimd", "sync")}
        self.waited = {}

    def op(self, eng_name, fn, waits=(), inc=None):
        real_waits = []
        key0 = eng_name
        for cnt, val in waits:
            if val <= 0:
                continue
            k = (key0, id(cnt))
            if self.waited.get(k, -1) >= val:
                continue
            self.waited[k] = val
            real_waits.append((cnt.h, val))
        incv = None
        if inc is not None:
            incv = (inc.h, inc.step)
            inc.n += inc.step

        def run(eng, fn=fn, real_waits=real_waits, incv=incv):
            for h, v in real_waits:
                eng.wait_ge(h, v)
            inst = fn(eng)
            if incv is not None:
                assert inst is not None
                inst.then_inc(incv[0], incv[1])

        self.streams[eng_name].append(run)


def build_program():
    nc = bass.Bass("TRN2", target_bir_lowering=False)

    xb = nc.dram_tensor("xb", [S, DM], F32, kind="ExternalInput").ap()
    xchunk = nc.dram_tensor("xchunk", [C, DM], F32, kind="ExternalInput").ap()
    wq_l = nc.dram_tensor("wq_l", [NH, 128, KC * 128], F32, kind="ExternalInput").ap()
    wkv = nc.dram_tensor("wkv", [DM, 256], F32, kind="ExternalInput").ap()
    wo_l = nc.dram_tensor("wo_l", [8, 128, KC * 512], F32, kind="ExternalInput").ap()
    w1l = nc.dram_tensor("w1l", [NB, 128, KC * 128], F32, kind="ExternalInput").ap()
    w3l = nc.dram_tensor("w3l", [NB, 128, KC * 128], F32, kind="ExternalInput").ap()
    w2t = nc.dram_tensor("w2t", [I, DM], F32, kind="ExternalInput").ap()
    sc1 = nc.dram_tensor("sc1", [128, NB * KC], F32, kind="ExternalInput").ap()
    sc3 = nc.dram_tensor("sc3", [128, NB * KC], F32, kind="ExternalInput").ap()
    sc2 = nc.dram_tensor("sc2", [128, NB * KC], F32, kind="ExternalInput").ap()
    dlt = nc.dram_tensor("dlt", [128, 4096], F32, kind="ExternalInput").ap()
    identb = nc.dram_tensor("identb", [128, 128], BF16, kind="ExternalInput").ap()
    identh = nc.dram_tensor("identh", [128, 128], F16, kind="ExternalInput").ap()
    qscl = nc.dram_tensor("qscl", [128, 1], F32, kind="ExternalInput").ap()

    att_h = nc.dram_tensor("att_h", [DM, C], BF16, kind="Internal").ap()
    s_h = nc.dram_tensor("s_h", [C, DM], BF16, kind="Internal").ap()
    y_h = nc.dram_tensor("y_h", [C, DM], F32, kind="Internal").ap()
    out = nc.dram_tensor("out", [C, DM], F32, kind="ExternalOutput").ap()

    wkv_v = wkv.rearrange("(c p) n -> p c n", p=128)
    w2t_v = w2t.rearrange("(c p) n -> p c n", p=128)
    att_hv = att_h.rearrange("(c p) t -> p c t", p=128)

    em = Emit()

    with ExitStack() as top:
        sem = lambda name, step: Cnt(top.enter_context(nc.semaphore(name)), step)
        qconst = sem("qconst", 16)
        qmisc = sem("qmisc", 16)
        qx = [sem(f"qx{i}", 16) for i in range(2)]
        qwq = [sem(f"qwq{i}", 16) for i in range(2)]
        qwo = [sem(f"qwo{i}", 16) for i in range(2)]
        qatt = [sem(f"qatt{i}", 16) for i in range(2)]
        qs = [sem(f"qs{i}", 16) for i in range(2)]
        qscs = [sem(f"qscs{i}", 16) for i in range(2)]
        qw13 = [sem(f"qw13{i}", 16) for i in range(2)]
        qw2 = [sem(f"qw2{i}", 16) for i in range(2)]
        qy = [sem(f"qy{i}", 16) for i in range(2)]
        qc = [sem(f"qc{i}", 16) for i in range(2)]
        qout = [sem(f"qout{i}", 16) for i in range(2)]
        allq = [qconst, qmisc] + qx + qwq + qwo + qatt + qs + qscs + qw13 + \
            qw2 + qy + qc + qout
        spe = sem("spe", 1)
        sve = sem("sve", 1)
        sse = sem("sse", 1)
        sge = sem("sge", 1)

        SB = lambda st, name, shape, dt: st.enter_context(nc.sbuf_tensor(name, shape, dt))
        PS = lambda st, name, shape, dt: st.enter_context(nc.psum_tensor(name, shape, dt))

        idb_sb = SB(top, "idb_sb", [128, 128], BF16)
        qscl_sb = SB(top, "qscl_sb", [128, 1], F32)
        idh_sb = SB(top, "idh_sb", [128, 128], F16)
        xnst = ExitStack()
        xnT = SB(xnst, "xnT", [128, KC * 1024], BF16)
        ph01 = ExitStack()
        K_sb = SB(ph01, "K_sb", [128, S], BF16)
        V_sb = SB(ph01, "V_sb", [128, KC * 128], BF16)
        dlt_sb = SB(ph01, "dlt_sb", [128, 4096], F32)

        ph0 = ExitStack()
        xbuf = [SB(ph0, f"xbuf{m}", [128, DM], F32) for m in range(2)]
        sqbuf = SB(ph0, "sqbuf", [128, DM], F32)
        xn_sb = [SB(ph0, f"xn{m}", [128, DM], BF16) for m in range(2)]
        xnt_tmp = [SB(ph0, f"xnt_tmp{m}", [128, KC * 128], BF16) for m in range(2)]
        wkv_sb = SB(ph0, "wkv_sb", [128, KC * 256], BF16)
        stats = {}
        for nm in ("sumx", "sumsq", "negmean", "var", "vareps", "std", "rstd",
                   "negmr"):
            stats[nm] = [SB(ph0, f"{nm}{m}", [128, 1], F32) for m in range(2)]
        tr_ps = [PS(ph0, f"tr_ps{m}", [128, 1024], BF16) for m in range(2)]
        k_ps = [PS(ph0, f"k_ps{m}", [128, 512], F32) for m in range(2)]
        v_ps = [PS(ph0, f"v_ps{m}", [128, 512], F32) for m in range(2)]

        em.op("gpsimd", lambda g: g.dma_start(
            wkv_sb[:].rearrange("p (c n) -> p c n", c=KC), wkv_v),
            inc=qconst)
        em.op("gpsimd", lambda g: g.dma_start(dlt_sb[:], dlt), inc=qconst)
        em.op("gpsimd", lambda g: g.dma_start(idb_sb[:], identb), inc=qconst)
        em.op("gpsimd", lambda g: g.dma_start(idh_sb[:], identh), inc=qconst)
        em.op("gpsimd", lambda g: g.dma_start(qscl_sb[:], qscl), inc=qconst)
        q_const = (qconst, qconst.n)

        mark = {}

        def phase0_tile(i, src_rows, own):
            m = i % 2
            waits = []
            if mark.get(("xapply", i - 2)) is not None:
                waits.append((sse, mark[("xapply", i - 2)]))
            em.op("sync", lambda sy, sr=src_rows: sy.dma_start(xbuf[m][:], sr),
                  waits=waits, inc=qx[m])
            q_x = (qx[m], qx[m].n)
            waits = [q_x]
            if ("stats", i - 2) in mark:
                waits.append((sve, mark[("stats", i - 2)]))
            em.op("scalar", lambda s: s.activation(
                sqbuf[:], xbuf[m][:], AF.Square, accum_out=stats["sumsq"][m][:]),
                waits=waits, inc=sse)
            a0 = sse.n
            em.op("vector", lambda v: v.tensor_reduce(
                stats["sumx"][m][:], xbuf[m][:], AX.X, AluOpType.add),
                waits=waits, inc=sve)
            g1 = sve.n
            em.op("vector", lambda v: v.tensor_scalar(
                stats["negmean"][m][:], stats["sumx"][m][:], -1.0 / DM, None,
                AluOpType.mult), waits=[(sve, g1)], inc=sve)
            g2 = sve.n
            em.op("vector", lambda v: v.scalar_tensor_tensor(
                stats["var"][m][:], stats["sumx"][m][:], stats["negmean"][m][:],
                stats["sumsq"][m][:], AluOpType.mult, AluOpType.add),
                waits=[(sve, g2), (sse, a0)], inc=sve)
            g3 = sve.n
            em.op("vector", lambda v: v.tensor_scalar(
                stats["vareps"][m][:], stats["var"][m][:], 1.0 / DM, EPS,
                AluOpType.mult, AluOpType.add), waits=[(sve, g3)], inc=sve)
            g3b = sve.n
            em.op("scalar", lambda s: s.activation(
                stats["std"][m][:], stats["vareps"][m][:], AF.Sqrt),
                waits=[(sve, g3b)], inc=sse)
            a1 = sse.n
            em.op("vector", lambda v: v.reciprocal(
                stats["rstd"][m][:], stats["std"][m][:]), waits=[(sse, a1)], inc=sve)
            g4 = sve.n
            em.op("vector", lambda v: v.tensor_scalar(
                stats["negmr"][m][:], stats["sumx"][m][:], stats["rstd"][m][:],
                -1.0 / DM, AluOpType.mult, AluOpType.mult),
                waits=[(sve, g4)], inc=sve)
            g5 = sve.n
            mark[("stats", i)] = g5
            waits = [(sve, g5)]
            if ("tr", i - 2, 3) in mark:
                waits.append((spe, mark[("tr", i - 2, 3)]))
            em.op("scalar", lambda s: s.activation(
                xn_sb[m][:], xbuf[m][:], AF.Identity,
                bias=stats["negmr"][m][:], scale=stats["rstd"][m][:]),
                waits=waits, inc=sse)
            a2 = sse.n
            mark[("xapply", i)] = a2
            for r in range(4):
                mb2 = r % 2
                waits = [(sse, a2), q_const]
                if ("trevac", i, r - 2) in mark:
                    waits.append(mark[("trevac", i, r - 2)])
                elif ("trevac", i - 1, r + 2) in mark:
                    waits.append(mark[("trevac", i - 1, r + 2)])
                for cc in range(8):
                    c = r * 8 + cc
                    em.op("tensor", lambda t, c=c, cc=cc, mb2=mb2, m=m: t.matmul(
                        tr_ps[mb2][:, cc * 128:(cc + 1) * 128],
                        xn_sb[m][:, c * 128:(c + 1) * 128], idb_sb[:],
                        start=(cc == 0), stop=(cc == 7), is_transpose=True),
                        waits=(waits if cc == 0 else ()),
                        inc=(spe if cc == 7 else None))
                mark[("tr", i, r)] = spe.n
                if own:
                    dst = xnT[:].rearrange("p (c t) -> p c t", t=1024)[
                        :, r * 8:(r + 1) * 8, (i % 8) * 128:((i % 8) + 1) * 128]
                else:
                    dst = xnt_tmp[m][:, r * 1024:(r + 1) * 1024].rearrange(
                        "p (c t) -> p c t", t=128)
                em.op("vector", lambda v, dst=dst, mb2=mb2: v.tensor_copy(
                    dst, tr_ps[mb2][:].rearrange("p (c t) -> p c t", t=128)),
                    waits=[(spe, mark[("tr", i, r)])], inc=sve)
                mark[("trevac", i, r)] = (sve, sve.n)
            if own:
                return
            waits = [mark[("trevac", i, 3)], q_const]
            if ("kve", i - 2) in mark:
                waits.append((sve, mark[("kve", i - 2)]))
            for c in range(KC):
                src_c = xnt_tmp[m][:, c * 128:(c + 1) * 128]
                em.op("tensor", lambda t, c=c, src_c=src_c: t.matmul(
                    k_ps[m][:, 0:128], wkv_sb[:, c * 256:c * 256 + 128], src_c,
                    start=(c == 0), stop=(c == KC - 1)),
                    waits=(waits if c == 0 else ()),
                    inc=(spe if c == KC - 1 else None))
            for c in range(KC):
                src_c = xnt_tmp[m][:, c * 128:(c + 1) * 128]
                em.op("tensor", lambda t, c=c, src_c=src_c: t.matmul(
                    v_ps[m][:, 0:128], src_c, wkv_sb[:, c * 256 + 128:c * 256 + 256],
                    start=(c == 0), stop=(c == KC - 1)),
                    inc=(spe if c == KC - 1 else None))
            mark[("kv", i)] = spe.n
            em.op("vector", lambda v: v.tensor_copy(
                K_sb[:, i * 128:(i + 1) * 128], k_ps[m][:, 0:128]),
                waits=[(spe, mark[("kv", i)])])
            em.op("vector", lambda v: v.tensor_copy(
                V_sb[:, i * 128:(i + 1) * 128], v_ps[m][:, 0:128]), inc=sve)
            mark[("kve", i)] = sve.n

        for i in range(NT):
            phase0_tile(i, xb[i * 128:(i + 1) * 128, :], own=False)
        for i in range(NT, NT + OT):
            phase0_tile(i, xchunk[(i - NT) * 128:(i - NT + 1) * 128, :], own=True)

        ph0_done_se = sse.n
        ph0_done_ve = sve.n
        ph0_done_pe = spe.n
        ph0.close()

        ph1 = ExitStack()
        wq_sb = [SB(ph1, f"wq{m}", [128, KC * 128], BF16) for m in range(2)]
        qT_sb = [SB(ph1, f"qT{m}", [128, C], BF16) for m in range(2)]
        t_sb = [SB(ph1, f"t{m}", [128, S], F16) for m in range(2)]
        p_sb = [SB(ph1, f"pp{m}", [128, S], F16) for m in range(2)]
        ptA = SB(ph1, "ptA", [128, 16 * 512], BF16)
        ptB = SB(ph1, "ptB", [128, 32 * 512], BF16)
        negmx = [SB(ph1, f"negmx{m}", [128, 1], F32) for m in range(2)]
        mx8 = [SB(ph1, f"mx8{m}", [128, 8], F32) for m in range(2)]
        den = [SB(ph1, f"den{m}", [128, 1], F32) for m in range(2)]
        rec = [SB(ph1, f"rec{m}", [128, 1], F32) for m in range(2)]
        d_sb = [SB(ph1, f"dd{m}", [128, 128], F16) for m in range(2)]
        att_st = [SB(ph1, f"attst{m}", [128, C], BF16) for m in range(2)]
        q_ps = [PS(ph1, f"q_ps{m}", [128, 512], F32) for m in range(2)]
        sc_ps = [PS(ph1, f"sc_ps{m}", [128, 512], F32) for m in range(2)]
        pt_ps = [PS(ph1, f"pt_ps{m}", [128, 512], F32) for m in range(2)]
        av_ps = [PS(ph1, f"av_ps{m}", [128, 512], F32) for m in range(2)]

        em.op("vector", lambda v: v.memset(ptA[:], 0.0),
              waits=[(sse, ph0_done_se), (sve, ph0_done_ve), (spe, ph0_done_pe)])
        em.op("vector", lambda v: v.memset(ptB[:], 0.0), inc=sve)
        pt_zeroed = sve.n

        def emit_head(h):
            m = h % 2
            if h < 2:
                waits = [(sse, ph0_done_se), (sve, ph0_done_ve),
                         (spe, ph0_done_pe)]
            else:
                waits = [(spe, mark[("qp", h - 2)])]
            em.op("gpsimd", lambda g: g.dma_start(
                wq_sb[m][:], wq_l[h]), waits=waits, inc=qwq[m])
            mark[("wq", h)] = (qwq[m], qwq[m].n)

        def emit_qp_chunk(h, c0, c1):
            m = h % 2
            waits = []
            if c0 == 0:
                waits = [mark[("wq", h)], (sse, ph0_done_se),
                         (sve, ph0_done_ve)]
                if ("qe", h - 1) in mark:
                    waits.append(mark[("qe", h - 1)])
            first = True
            for c in range(c0, c1):
                for half in range(2):
                    em.op("tensor", lambda t, half=half, c=c, m=m: t.matmul(
                        q_ps[half][:],
                        wq_sb[m][:, c * 128:(c + 1) * 128],
                        xnT[:, c * 1024 + half * 512: c * 1024 + half * 512 + 512],
                        start=(c == 0), stop=(c == KC - 1)),
                        waits=(waits if first else ()),
                        inc=(spe if (c == c1 - 1 and half == 1) else None))
                    first = False
            if c1 == KC:
                mark[("qp", h)] = spe.n

        def emit_qe(h):
            m = h % 2
            for half in range(2):
                em.op("scalar", lambda s, half=half, m=m: s.activation(
                    qT_sb[m][:, half * 512:(half + 1) * 512], q_ps[half][:],
                    AF.Identity, scale=qscl_sb[:]),
                    waits=([(spe, mark[("qp", h)]), q_const]
                           if half == 0 else ()),
                    inc=(sse if half == 1 else None))
            mark[("qe", h)] = (sse, sse.n)

        def emit_head_body(h):
            m = h % 2
            slope = SLOPES[h]
            if h == 0:
                emit_qe(0)
            def emit_sc(qt):
                for kb in range(qt + 1):
                    emit_sc_kb(qt, kb)
                emit_mx(qt)

            def emit_sc_kb(qt, kb):
                mq = qt % 2
                if True:
                    bank = len(seq["stt"]) % 2
                    waits = [mark[("qe", h)], (sve, pt_zeroed)]
                    if len(seq["stt"]) >= 2:
                        waits.append((sve, seq["stt"][-2]))
                    em.op("tensor", lambda t, qt=qt, kb=kb, bank=bank: t.matmul(
                        sc_ps[bank][:],
                        qT_sb[m][:, qt * 128:(qt + 1) * 128],
                        K_sb[:, kb * 512:(kb + 1) * 512],
                        start=True, stop=True),
                        waits=waits, inc=spe)
                    mark[("sc", h, qt, kb)] = spe.n
                    off = (kb - qt) * 512 + 3584
                    waits = [(spe, mark[("sc", h, qt, kb)])]
                    if kb == 0:
                        pe_exp = mark.get(("exp", h, qt - 2)) or mark.get(
                            ("exp", h - 1, qt + QTN - 2))
                        if pe_exp:
                            waits.append((sse, pe_exp))
                    em.op("vector", lambda v, qt=qt, kb=kb, off=off, mq=mq,
                          bank=bank: v.scalar_tensor_tensor(
                              t_sb[mq][:, kb * 512:(kb + 1) * 512],
                              dlt_sb[:, off:off + 512], slope, sc_ps[bank][:],
                              AluOpType.mult, AluOpType.add),
                          waits=waits, inc=sve)
                    seq["stt"].append(sve.n)
                    em.op("vector", lambda v, qt=qt, kb=kb, mq=mq: v.tensor_reduce(
                        mx8[mq][:, kb:kb + 1],
                        t_sb[mq][:, kb * 512:(kb + 1) * 512],
                        AX.X, AluOpType.max),
                        waits=[(sve, seq["stt"][-1])], inc=sve)
                    mark[("mxp", h, qt, kb)] = sve.n

            def emit_mx(qt):
                mq = qt % 2
                em.op("vector", lambda v, mq=mq, qt=qt: v.tensor_reduce(
                    negmx[mq][:], mx8[mq][:, 0:qt + 1], AX.X,
                    AluOpType.max, negate=True),
                    waits=[(sve, mark[("mxp", h, qt, qt)])], inc=sve)
                mark[("mx", h, qt)] = sve.n
                waits = [(sve, mark[("mx", h, qt)])]
                pe_pt = mark.get(("ptdone", h, qt - 2)) or mark.get(
                    ("ptdone", h - 1, qt + QTN - 2))
                if pe_pt:
                    waits.append((spe, pe_pt))
                em.op("scalar", lambda s, mq=mq, qt=qt: s.activation(
                    p_sb[mq][:, 0:(qt + 1) * 512], t_sb[mq][:, 0:(qt + 1) * 512],
                    AF.Exp, bias=negmx[mq][:], accum_out=den[mq][:]),
                    waits=waits, inc=sse)
                mark[("exp", h, qt)] = sse.n

            def emit_rd(qt):
                mq = qt % 2
                em.op("vector", lambda v, mq=mq: v.reciprocal(
                    rec[mq][:], den[mq][:]),
                    waits=[(sse, mark[("exp", h, qt)])], inc=sve)
                rr = sve.n
                waits = [(sve, rr)]
                pe_pt = mark.get(("ptdone", h, qt - 2)) or mark.get(
                    ("ptdone", h - 1, qt + QTN - 2))
                if pe_pt:
                    waits.append((spe, pe_pt))
                em.op("vector", lambda v, mq=mq: v.tensor_scalar(
                    d_sb[mq][:], idh_sb[:], rec[mq][:], None, AluOpType.mult),
                    waits=waits, inc=sve)
                mark[("dd", h, qt)] = sve.n

            def emit_pt_one(qt, g):
                mq = qt % 2
                qg = qt // 4
                dstbuf = ptA if qg == 0 else ptB
                if True:
                    bank = len(seq["pte"]) % 2
                    waits = [(sse, mark[("exp", h, qt)]), (sve, mark[("dd", h, qt)])]
                    if len(seq["pte"]) >= 2:
                        waits.append((sse, seq["pte"][-2]))
                    for kk in range(4):
                        ks = g * 4 + kk
                        em.op("tensor", lambda t, ks=ks, kk=kk, mq=mq,
                              bank=bank: t.matmul(
                            pt_ps[bank][:, kk * 128:(kk + 1) * 128],
                            p_sb[mq][:, ks * 128:(ks + 1) * 128], d_sb[mq][:],
                            start=(kk == 0), stop=(kk == 3)),
                            waits=(waits if kk == 0 else ()),
                            inc=(spe if kk == 3 else None))
                    mark[("ptg", h, qt, g)] = spe.n
                    dst = dstbuf[:].rearrange("p (c q) -> p c q", q=512)[
                        :, g * 4:(g + 1) * 4, (qt % 4) * 128:((qt % 4) + 1) * 128]
                    waits = [(spe, mark[("ptg", h, qt, g)])]
                    if qt % 4 == 0 and g == 0:
                        prev_av = mark.get(("av", h, qt // 4 - 1)) or mark.get(
                            ("av", h - 1, qt // 4))
                        if prev_av:
                            waits.append((spe, prev_av))
                    em.op("scalar", lambda s, dst=dst, bank=bank: s.copy(
                        dst, pt_ps[bank][:].rearrange("p (c q) -> p c q", q=128)),
                        waits=waits, inc=sse)
                    seq["pte"].append(sse.n)
                if g == qt:
                    mark[("ptdone", h, qt)] = spe.n
                    mark[("ptedone", h, qt)] = sse.n

            def emit_av(qg):
                srcbuf = ptA if qg == 0 else ptB
                ncks = 16 * qg + 16
                waits = [(sse, mark[("ptedone", h, qg * 4 + 3)])]
                prev = mark.get(("ave", h, qg - 2)) or mark.get(("ave", h - 1, qg))
                if prev:
                    waits.append((sse, prev))
                for ck in range(ncks):
                    em.op("tensor", lambda t, ck=ck, qg=qg, srcbuf=srcbuf: t.matmul(
                        av_ps[qg % 2][:],
                        V_sb[:, ck * 128:(ck + 1) * 128],
                        srcbuf[:, ck * 512:(ck + 1) * 512],
                        start=(ck == 0), stop=(ck == ncks - 1)),
                        waits=(waits if ck == 0 else ()),
                        inc=(spe if ck == ncks - 1 else None))
                mark[("av", h, qg)] = spe.n
                waits = [(spe, mark[("av", h, qg)])]
                if ("att_store", h - 2) in mark:
                    waits.append(mark[("att_store", h - 2)])
                em.op("scalar", lambda s, qg=qg: s.copy(
                    att_st[m][:, qg * 512:(qg + 1) * 512], av_ps[qg % 2][:]),
                    waits=waits, inc=sse)
                mark[("ave", h, qg)] = sse.n

            emit_sc(0)
            for qt in range(1, QTN):
                ptg_next = 0
                for kb in range(qt + 1):
                    emit_sc_kb(qt, kb)
                    if kb == min(1, qt):
                        emit_rd(qt - 1)
                    if kb >= 3 and ptg_next < qt:
                        emit_pt_one(qt - 1, ptg_next)
                        ptg_next += 1
                while ptg_next < qt:
                    emit_pt_one(qt - 1, ptg_next)
                    ptg_next += 1
                emit_mx(qt)
                if qt - 1 == 3:
                    emit_av(0)
                if qt >= 4 and h + 1 < NH:
                    emit_qp_chunk(h + 1, (qt - 4) * 8, (qt - 3) * 8)
                    if qt == QTN - 1:
                        emit_qe(h + 1)
            emit_rd(QTN - 1)
            for g in range(QTN):
                emit_pt_one(QTN - 1, g)
            emit_av(1)
            em.op("sync", lambda sy: sy.dma_start(
                att_h[h * 128:(h + 1) * 128, :], att_st[m][:]),
                waits=[(sse, mark[("ave", h, 1)])], inc=qatt[m])
            mark[("att_store", h)] = (qatt[m], qatt[m].n)

        seq = {"stt": [], "pte": []}
        emit_head(0)
        emit_head(1)
        emit_qp_chunk(0, 0, KC)
        for h in range(NH):
            if h + 2 < NH:
                emit_head(h + 2)
            emit_head_body(h)

        ph1_stores = [(qatt[0], qatt[0].n), (qatt[1], qatt[1].n)]
        ph1.close()
        ph01.close()

        ph2 = ExitStack()
        att_all = SB(ph2, "att_all", [128, KC * 1024], BF16)
        wo_sb = [SB(ph2, f"wo{m}", [128, KC * 512], BF16) for m in range(2)]
        s_st = [SB(ph2, f"s_st{m}", [128, 512], BF16) for m in range(2)]
        s_ps = [PS(ph2, f"s_ps{m}", [128, 512], F32) for m in range(2)]

        em.op("sync", lambda sy: sy.dma_start(
            att_all[:].rearrange("p (c t) -> p c t", t=1024), att_hv),
            waits=ph1_stores, inc=qmisc)
        q_attall = (qmisc, qmisc.n)

        def emit_wo_load(dt):
            mw = dt % 2
            if dt < 2:
                waits = list(ph1_stores)
            else:
                waits = [(spe, mark[("wopd", dt - 2)])]
            em.op("gpsimd", lambda g, dt=dt, mw=mw: g.dma_start(
                wo_sb[mw][:], wo_l[dt]), waits=waits, inc=qwo[mw])
            mark[("wol", dt)] = (qwo[mw], qwo[mw].n)

        emit_wo_load(0)
        emit_wo_load(1)
        for dt in range(8):
            mw = dt % 2
            for qt in range(QTN):
                waits = [mark[("wol", dt)], q_attall]
                prev = mark.get(("se2", dt, qt - 2)) or mark.get(("se2", dt - 1, qt + 6))
                if prev:
                    waits.append((sse, prev))
                for hc in range(KC):
                    em.op("tensor", lambda t, dt=dt, qt=qt, hc=hc, mw=mw: t.matmul(
                        s_ps[qt % 2][:],
                        att_all[:, hc * 1024 + qt * 128: hc * 1024 + qt * 128 + 128],
                        wo_sb[mw][:, hc * 512:(hc + 1) * 512],
                        start=(hc == 0), stop=(hc == KC - 1)),
                        waits=(waits if hc == 0 else ()),
                        inc=(spe if hc == KC - 1 else None))
                mark[("wopq", dt, qt)] = spe.n
                waits = [(spe, mark[("wopq", dt, qt)])]
                prev = mark.get(("s_store", dt, qt - 2)) or mark.get(
                    ("s_store", dt - 1, qt + 6))
                if prev:
                    waits.append(prev)
                em.op("scalar", lambda s, qt=qt: s.copy(
                    s_st[qt % 2][:], s_ps[qt % 2][:]), waits=waits, inc=sse)
                mark[("se2", dt, qt)] = sse.n
                em.op("sync", lambda sy, dt=dt, qt=qt: sy.dma_start(
                    s_h[qt * 128:(qt + 1) * 128, dt * 512:(dt + 1) * 512],
                    s_st[qt % 2][:]),
                    waits=[(sse, mark[("se2", dt, qt)])], inc=qs[qt % 2])
                mark[("s_store", dt, qt)] = (qs[qt % 2], qs[qt % 2].n)
            mark[("wopd", dt)] = spe.n
            if dt + 2 < 8:
                emit_wo_load(dt + 2)

        s_stores = [(qs[0], qs[0].n), (qs[1], qs[1].n)]
        ph2.close()

        ph3 = ExitStack()
        scs_sb = [SB(ph3, f"scs{m}", [128, 3 * KC], F32) for m in range(2)]
        w1r = [SB(ph3, f"w1r{m}", [128, KC * 128], BF16) for m in range(2)]
        w3r = [SB(ph3, f"w3r{m}", [128, KC * 128], BF16) for m in range(2)]
        w2g = SB(ph3, "w2g", [128, G2 * DM], BF16)
        hT = SB(ph3, "hT", [128, G2 * 1024], BF16)
        gs_tmp = SB(ph3, "gs_tmp", [128, 1024], F32)
        y_st = [SB(ph3, f"y_st{m}", [128, 2048], F32) for m in range(2)]
        g_ps = PS(ph3, "g_ps", [128, 1024], F32)
        u_ps = PS(ph3, "u_ps", [128, 1024], F32)
        y_ps = [PS(ph3, f"y_ps{m}", [128, 512], F32) for m in range(2)]
        y_ps2 = [PS(ph3, f"y_ps2{m}", [128, 512], F32) for m in range(2)]

        def dq_fn(v, buf, scoff, msc, nb):
            t3 = buf.rearrange("p (c n) -> p c n", n=128)
            scb = scs_sb[msc][:, scoff * KC:(scoff + 1) * KC].unsqueeze(
                2).broadcast_to([128, KC, 128])
            return v.tensor_tensor(t3, t3, scb, AluOpType.mult)

        def emit_w13(nb):
            m = nb % 2
            waits = []
            if ("f1h", nb - 2, 1) in mark:
                waits.append((spe, mark[("f1h", nb - 2, 1)]))
            else:
                waits.extend(s_stores)
            swaits = list(waits)
            if ("dq2", nb - 2) in mark:
                swaits.append((sve, mark[("dq2", nb - 2)]))
            em.op("sync", lambda sy, nb=nb, m=m: sy.dma_start(
                scs_sb[m][:, 0:KC], sc1[:, nb * KC:(nb + 1) * KC]),
                waits=swaits, inc=qscs[m])
            em.op("sync", lambda sy, nb=nb, m=m: sy.dma_start(
                scs_sb[m][:, KC:2 * KC], sc3[:, nb * KC:(nb + 1) * KC]),
                inc=qscs[m])
            em.op("sync", lambda sy, nb=nb, m=m: sy.dma_start(
                scs_sb[m][:, 2 * KC:3 * KC], sc2[:, nb * KC:(nb + 1) * KC]),
                inc=qscs[m])
            mark[("scl", nb)] = (qscs[m], qscs[m].n)
            em.op("gpsimd", lambda g, nb=nb, m=m: g.dma_start(
                w1r[m][:], w1l[nb]), waits=waits, inc=qw13[m])
            em.op("gpsimd", lambda g, nb=nb, m=m: g.dma_start(
                w3r[m][:], w3l[nb]), inc=qw13[m])
            q_w = (qw13[m], qw13[m].n)
            em.op("vector", lambda v, m=m, nb=nb: dq_fn(v, w1r[m][:], 0, m, nb),
                  waits=[q_w, mark[("scl", nb)]])
            em.op("vector", lambda v, m=m, nb=nb: dq_fn(v, w3r[m][:], 1, m, nb),
                  inc=sve)
            mark[("dq13", nb)] = sve.n

        def emit_w2(nb):
            gidx = nb // G2
            slot = nb % G2
            m = nb % 2
            waits = []
            if ("f2", gidx - 1) in mark:
                waits.append((spe, mark[("f2", gidx - 1)]))
            elif gidx == 0:
                waits.extend(s_stores)
            if ("dq2", nb - 2) in mark:
                waits.append((sve, mark[("dq2", nb - 2)]))
            em.op("gpsimd", lambda g, nb=nb, slot=slot: g.dma_start(
                w2g[:, slot * DM:(slot + 1) * DM], w2t_v[:, nb, :]),
                waits=waits, inc=qw2[m])
            q_w = (qw2[m], qw2[m].n)
            em.op("vector", lambda v, slot=slot, nb=nb, m=m: dq_fn(
                v, w2g[:, slot * DM:(slot + 1) * DM], 2, m, nb),
                waits=[q_w, mark[("scl", nb)]], inc=sve)
            mark[("dq2", nb)] = sve.n

        emit_w13(0)
        emit_w2(0)
        for nb in range(NB):
            m = nb % 2
            gidx = nb // G2
            slot = nb % G2
            if nb + 1 < NB:
                emit_w13(nb + 1)
            for half in range(2):
                waits = [(sve, mark[("dq13", nb)]), (sse, ph0_done_se),
                         (sve, ph0_done_ve)]
                if ("hm", nb - 1, half) in mark:
                    waits.append((sve, mark[("hm", nb - 1, half)]))
                if slot == 0 and half == 0 and ("ye", gidx - 1, 7, 7) in mark:
                    waits.append((sse, mark[("ye", gidx - 1, 7, 7)]))
                first = True
                for c in range(KC):
                    em.op("tensor", lambda t, c=c, half=half, m=m: t.matmul(
                        g_ps[:, half * 512:(half + 1) * 512],
                        w1r[m][:, c * 128:(c + 1) * 128],
                        xnT[:, c * 1024 + half * 512: c * 1024 + half * 512 + 512],
                        start=(c == 0), stop=(c == KC - 1)),
                        waits=(waits if first else ())); first = False
                    em.op("tensor", lambda t, c=c, half=half, m=m: t.matmul(
                        u_ps[:, half * 512:(half + 1) * 512],
                        w3r[m][:, c * 128:(c + 1) * 128],
                        xnT[:, c * 1024 + half * 512: c * 1024 + half * 512 + 512],
                        start=(c == 0), stop=(c == KC - 1)),
                        inc=(spe if c == KC - 1 else None))
                mark[("f1h", nb, half)] = spe.n
                em.op("scalar", lambda s, half=half: s.activation(
                    gs_tmp[:, half * 512:(half + 1) * 512],
                    g_ps[:, half * 512:(half + 1) * 512], AF.Silu),
                    waits=[(spe, mark[("f1h", nb, half)])], inc=sse)
                mark[("si", nb, half)] = sse.n
                em.op("vector", lambda v, slot=slot, half=half: v.tensor_tensor(
                    hT[:, slot * 1024 + half * 512: slot * 1024 + half * 512 + 512],
                    gs_tmp[:, half * 512:(half + 1) * 512],
                    u_ps[:, half * 512:(half + 1) * 512],
                    AluOpType.mult), waits=[(sse, mark[("si", nb, half)])],
                    inc=sve)
                mark[("hm", nb, half)] = sve.n
            if slot == G2 - 1:
                g = gidx
                ybank = [g_ps[:, 0:512], g_ps[:, 512:1024],
                         u_ps[:, 0:512], u_ps[:, 512:1024],
                         y_ps[0][:], y_ps[1][:], y_ps2[0][:], y_ps2[1][:]]
                for tb in range(8):
                    for nn in range(G2):
                        for dt in range(8):
                            waits = []
                            if nn == 0:
                                waits = [(sve, mark[("hm", nb, 1)]),
                                         (sve, mark[("dq2", nb)])]
                                ye_prev = mark.get(("ye", g, tb - 1, dt)) or \
                                    mark.get(("ye", g - 1, 7, dt))
                                if ye_prev:
                                    waits.append((sse, ye_prev))
                            em.op("tensor", lambda t, tb=tb, dt=dt, nn=nn: t.matmul(
                                ybank[dt],
                                hT[:, nn * 1024 + tb * 128: nn * 1024 + tb * 128 + 128],
                                w2g[:, nn * DM + dt * 512: nn * DM + (dt + 1) * 512],
                                start=(nn == 0), stop=(nn == G2 - 1)),
                                waits=waits,
                                inc=(spe if (nn == G2 - 1 and dt == 7) else None))
                    mark[("f2t", g, tb)] = spe.n
                    for dt in range(8):
                        half = dt // 4
                        ybuf = (tb * 2 + half) % 2
                        waits = [(spe, mark[("f2t", g, tb)])]
                        if dt % 4 == 0:
                            prev = mark.get(("y_store", g, tb - 1, half)) or \
                                mark.get(("y_store", g - 1, 7, half))
                            if prev:
                                waits.append(prev)
                        em.op("scalar", lambda s, dt=dt, ybuf=ybuf: s.copy(
                            y_st[ybuf][:, (dt % 4) * 512:((dt % 4) + 1) * 512],
                            ybank[dt]), waits=waits, inc=sse)
                        mark[("ye", g, tb, dt)] = sse.n
                        if dt % 4 == 3:
                            em.op("gpsimd", lambda gp, tb=tb, g=g, half=half,
                                  ybuf=ybuf: gp.dma_start(
                                      y_h[tb * 128:(tb + 1) * 128,
                                          half * 2048:(half + 1) * 2048],
                                      y_st[ybuf][:],
                                      **({"accum_op": AluOpType.add}
                                         if g > 0 else {})),
                                  waits=[(sse, mark[("ye", g, tb, dt)])],
                                  inc=qy[ybuf])
                            mark[("y_store", g, tb, half)] = (qy[ybuf],
                                                              qy[ybuf].n)
                mark[("f2", g)] = spe.n
            if nb + 1 < NB:
                emit_w2(nb + 1)

        y_stores = [(qy[0], qy[0].n), (qy[1], qy[1].n)]
        ph3.close()
        xnst.close()

        ph4 = ExitStack()
        y_in = [SB(ph4, f"y_in{m}", [128, DM], F32) for m in range(2)]
        si_in = [SB(ph4, f"si_in{m}", [128, DM], BF16) for m in range(2)]
        x_in = [SB(ph4, f"x_in{m}", [128, DM], F32) for m in range(2)]
        gf_tmp = [SB(ph4, f"gf{m}", [128, DM], F32) for m in range(2)]
        sg_tmp = [SB(ph4, f"sg{m}", [128, DM], F32) for m in range(2)]
        out_t = [SB(ph4, f"out_t{m}", [128, DM], F32) for m in range(2)]

        for tb in range(OT):
            m = tb % 2
            waits = list(y_stores) + list(s_stores)
            if ("cmb", tb - 2) in mark:
                waits.append((sve, mark[("cmb", tb - 2)]))
            em.op("sync", lambda sy, tb=tb, m=m: sy.dma_start(
                y_in[m][:], y_h[tb * 128:(tb + 1) * 128, :]), waits=waits,
                inc=qc[m])
            em.op("sync", lambda sy, tb=tb, m=m: sy.dma_start(
                si_in[m][:], s_h[tb * 128:(tb + 1) * 128, :]), inc=qc[m])
            em.op("sync", lambda sy, tb=tb, m=m: sy.dma_start(
                x_in[m][:], xchunk[tb * 128:(tb + 1) * 128, :]), inc=qc[m])
            q_c = (qc[m], qc[m].n)
            waits = [q_c]
            if ("cmb_o", tb - 2) in mark:
                waits.append(mark[("cmb_o", tb - 2)])
            em.op("scalar", lambda s, m=m: s.activation(
                gf_tmp[m][:], y_in[m][:], AF.Silu), waits=waits, inc=sse)
            c1 = sse.n
            em.op("vector", lambda v, m=m: v.tensor_tensor(
                sg_tmp[m][:], si_in[m][:], gf_tmp[m][:], AluOpType.mult),
                waits=[(sse, c1)], inc=sve)
            c2 = sve.n
            em.op("vector", lambda v, m=m: v.tensor_tensor(
                out_t[m][:], sg_tmp[m][:], x_in[m][:], AluOpType.add),
                waits=[(sve, c2)], inc=sve)
            mark[("cmb", tb)] = sve.n
            em.op("sync", lambda sy, tb=tb, m=m: sy.dma_start(
                out[tb * 128:(tb + 1) * 128, :], out_t[m][:]),
                waits=[(sve, mark[("cmb", tb)])], inc=qout[m])
            mark[("cmb_o", tb)] = (qout[m], qout[m].n)
        ph4.close()

        totals = [(q.h, q.n) for q in allq if q.n > 0]

        def final_wait(eng):
            for hh, vv in totals:
                eng.wait_ge(hh, vv)
        em.streams["sync"].append(final_wait)

        with nc.Block() as blk:
            @blk.tensor
            def _(eng):
                for fn in em.streams["tensor"]:
                    fn(eng)

            @blk.vector
            def _(eng):
                for fn in em.streams["vector"]:
                    fn(eng)

            @blk.scalar
            def _(eng):
                for fn in em.streams["scalar"]:
                    fn(eng)

            @blk.gpsimd
            def _(eng):
                for fn in em.streams["gpsimd"]:
                    fn(eng)

            @blk.sync
            def _(eng):
                for fn in em.streams["sync"]:
                    fn(eng)

    return nc


def _host_inputs(L0_attn_hidden_states, L0_attn_Wqkv, L0_attn_Wo,
                 L0_ffn_w1, L0_ffn_w1_scale, L0_ffn_w3, L0_ffn_w3_scale,
                 L0_ffn_w2, L0_ffn_w2_scale):
    x = np.ascontiguousarray(np.asarray(L0_attn_hidden_states, dtype=np.float32))
    wqkv = np.asarray(L0_attn_Wqkv, dtype=np.float32)
    wo = np.asarray(L0_attn_Wo, dtype=np.float32)

    def tile_kn(wt, nblocks):
        K, N = wt.shape
        return np.ascontiguousarray(
            wt.reshape(K // 128, 128, nblocks, N // nblocks)
            .transpose(2, 1, 0, 3).reshape(nblocks, 128, -1))

    wq_l = tile_kn(wqkv[:, :4096], NH)
    wkv = np.ascontiguousarray(wqkv[:, 4096:])
    wo_l = tile_kn(wo, 8)
    w1l = tile_kn(np.asarray(L0_ffn_w1, dtype=np.float32).T, NB)
    w3l = tile_kn(np.asarray(L0_ffn_w3, dtype=np.float32).T, NB)
    w2t = np.ascontiguousarray(np.asarray(L0_ffn_w2, dtype=np.float32).T)
    s1 = np.asarray(L0_ffn_w1_scale, dtype=np.float32)
    s3 = np.asarray(L0_ffn_w3_scale, dtype=np.float32)
    s2 = np.asarray(L0_ffn_w2_scale, dtype=np.float32)
    sc1 = np.ascontiguousarray(np.broadcast_to(s1.reshape(1, -1), (128, NB * KC)))
    sc3 = np.ascontiguousarray(np.broadcast_to(s3.reshape(1, -1), (128, NB * KC)))
    sc2 = np.ascontiguousarray(np.broadcast_to(
        np.ascontiguousarray(s2.T).reshape(1, -1), (128, NB * KC)))
    identb = np.eye(128, dtype=ml_dtypes.bfloat16)
    identh = np.eye(128, dtype=np.float16)

    in_maps = []
    for core in range(8):
        b, j = core // 4, core % 4
        rows = np.concatenate([np.arange(512 * k + 128 * j, 512 * k + 128 * j + 128)
                               for k in range(QTN)])
        cc = np.arange(4096, dtype=np.float64)[None, :]
        pq = np.arange(128, dtype=np.float64)[:, None]
        v = (cc - 3584.0 - 128.0 * j) - pq
        tmpl = np.where(v <= 0, v, SENT).astype(np.float32)
        in_maps.append(dict(
            xb=np.ascontiguousarray(x[b]),
            xchunk=np.ascontiguousarray(x[b][rows]),
            wq_l=wq_l, wkv=wkv, wo_l=wo_l, w1l=w1l, w3l=w3l, w2t=w2t,
            sc1=sc1, sc3=sc3, sc2=sc2,
            dlt=tmpl, identb=identb, identh=identh,
            qscl=np.full((128, 1), QSCALE, dtype=np.float32),
        ))
    return in_maps


def kernel(L0_attn_hidden_states, L0_attn_Wqkv, L0_attn_Wo,
           L0_ffn_w1, L0_ffn_w1_scale, L0_ffn_w3, L0_ffn_w3_scale,
           L0_ffn_w2, L0_ffn_w2_scale):
    in_maps = _host_inputs(L0_attn_hidden_states, L0_attn_Wqkv, L0_attn_Wo,
                           L0_ffn_w1, L0_ffn_w1_scale, L0_ffn_w3, L0_ffn_w3_scale,
                           L0_ffn_w2, L0_ffn_w2_scale)
    nc = build_program()
    res = run_bass_kernel_spmd(nc, in_maps, core_ids=list(range(8)))
    out = np.empty((B, S, DM), dtype=np.float32)
    for core in range(8):
        b, j = core // 4, core % 4
        rows = np.concatenate([np.arange(512 * k + 128 * j, 512 * k + 128 * j + 128)
                               for k in range(QTN)])
        out[b][rows] = np.asarray(res.results[core]["out"], dtype=np.float32)
    return out
